# version 09
# baseline (speedup 1.0000x reference)
import contextlib
import numpy as np
import ml_dtypes
import concourse.bass as bass
import concourse.mybir as mybir
from concourse.bass_utils import run_bass_kernel_spmd

F32 = mybir.dt.float32
BF16 = mybir.dt.bfloat16
AF = mybir.ActivationFunctionType
ALU = mybir.AluOpType
BF = ml_dtypes.bfloat16

D = 4096
S = 4096
B = 2
TPC = 1024
HEAD = 128
NKV = 8
NQH = 32
QPG = 4
BLOCK = 64
LOCAL = 16
VSTRIDE = 8
SM = 1.0 / 128.0
EPS = 1e-5
NBLK = S // BLOCK
NB_PAD = 36
NK = NB_PAD * BLOCK
NKT = NK // 128
NC_ = 8


def allowed(qb, kb):
    return kb <= qb and (qb - kb < LOCAL or (kb + 1) % VSTRIDE == 0)


def needed_blocks(q):
    s = set()
    for qb in range(16 * q, 16 * q + 16):
        for kb in range(NBLK):
            if allowed(qb, kb):
                s.add(kb)
    return sorted(s)


def build_schedule():
    used = {0: set(), 1: set()}
    for q in range(4):
        nb = needed_blocks(q)
        for t in range(2):
            qb0 = 16 * q + 8 * t
            for j in range(NKT):
                for half in range(2):
                    cb = 2 * j + half
                    if cb >= len(nb):
                        continue
                    kb = nb[cb]
                    if any(allowed(qb, kb) for qb in range(qb0, qb0 + 8)):
                        used[t].add(j)
    return {t: sorted(used[t]) for t in range(2)}


SCHED = build_schedule()
NT0 = len(SCHED[0])
NT1 = len(SCHED[1])
NTILES = NT0 + NT1


def core_tables(q):
    nb = list(needed_blocks(q))
    nreal = len(nb)
    nb_pad = nb + [0] * (NB_PAD - nreal)
    kpos = np.zeros(NK, dtype=np.int64)
    for cb, kb in enumerate(nb_pad):
        kpos[cb * BLOCK:(cb + 1) * BLOCK] = kb * BLOCK + np.arange(BLOCK)
    return nb_pad, nreal, kpos


def mask_blob(q):
    nb_pad, nreal, kpos = core_tables(q)
    tiles = []
    for t in range(2):
        qp0 = q * 1024 + t * 512
        qpos = qp0 + np.arange(512)
        qb = qpos // BLOCK
        for j in SCHED[t]:
            m = np.zeros((128, 512), dtype=np.float32)
            for kk in range(128):
                ci = j * 128 + kk
                cb = ci // BLOCK
                if cb >= nreal:
                    continue
                kb = nb_pad[cb]
                kp = kpos[ci]
                ok = np.array([allowed(int(x), kb) for x in qb]) & (qpos >= kp)
                m[kk, :] = ok.astype(np.float32)
            tiles.append(m)
    return np.concatenate(tiles, axis=1).astype(BF)


def rope_tables():
    inv_freq = 1.0 / (1e6 ** (np.arange(0, HEAD, 2, dtype=np.float64) / HEAD))
    t = np.arange(S, dtype=np.float64)
    freqs = np.outer(t, inv_freq)
    emb = np.concatenate([freqs, freqs], axis=-1)
    cos = np.cos(emb).T.astype(np.float32)
    sin = np.sin(emb).T.astype(np.float32)
    sgn = np.ones((128, 1), dtype=np.float32)
    sgn[:64] = -1.0
    return cos, sin * sgn


class Sem:
    def __init__(self, st, nc, name):
        self.h = st.enter_context(nc.semaphore(name))
        self.n = 0

    def inc(self, instr, k=1):
        instr.then_inc(self.h, k)
        self.n += k
        return self.n


def build_nc():
    nc = bass.Bass("TRN2", target_bir_lowering=False)
    xneed = nc.dram_tensor("xneed", [D, NK], F32, kind="ExternalInput").ap()
    xown = nc.dram_tensor("xown", [D, TPC], F32, kind="ExternalInput").ap()
    wqT = nc.dram_tensor("wqT", [D, 4096], F32, kind="ExternalInput").ap()
    wkT = nc.dram_tensor("wkT", [D, 1024], F32, kind="ExternalInput").ap()
    wvT = nc.dram_tensor("wvT", [D, 1024], F32, kind="ExternalInput").ap()
    wdT = nc.dram_tensor("wdT", [D, 4096], F32, kind="ExternalInput").ap()
    WTt = nc.dram_tensor("WTt", [D, 4096], F32, kind="ExternalInput").ap()
    ATi = nc.dram_tensor("ATi", [D, 128], F32, kind="ExternalInput").ap()
    Bti = nc.dram_tensor("Bti", [128, 4096], F32, kind="ExternalInput").ap()
    lwI = nc.dram_tensor("lwI", [8, 1], F32, kind="ExternalInput").ap()
    selI = nc.dram_tensor("selI", [8, 128], F32, kind="ExternalInput").ap()
    bqI = nc.dram_tensor("bqI", [128, 32], F32, kind="ExternalInput").ap()
    bkI = nc.dram_tensor("bkI", [128, 8], F32, kind="ExternalInput").ap()
    bvI = nc.dram_tensor("bvI", [128, 8], F32, kind="ExternalInput").ap()
    bdI = nc.dram_tensor("bdI", [128, 32], F32, kind="ExternalInput").ap()
    bfI = nc.dram_tensor("bfI", [128, 32], F32, kind="ExternalInput").ap()
    cosqI = nc.dram_tensor("cosqI", [128, TPC], BF16, kind="ExternalInput").ap()
    sinqI = nc.dram_tensor("sinqI", [128, TPC], BF16, kind="ExternalInput").ap()
    coskI = nc.dram_tensor("coskI", [128, NK], BF16, kind="ExternalInput").ap()
    sinkI = nc.dram_tensor("sinkI", [128, NK], BF16, kind="ExternalInput").ap()
    masksI = nc.dram_tensor("masksI", [128, NTILES * 512], BF16, kind="ExternalInput").ap()
    onesbI = nc.dram_tensor("onesbI", [128, 1], BF16, kind="ExternalInput").ap()
    onesfI = nc.dram_tensor("onesfI", [1, 128], F32, kind="ExternalInput").ap()
    onesfbI = nc.dram_tensor("onesfbI", [1, 128], BF16, kind="ExternalInput").ap()
    identI = nc.dram_tensor("identI", [128, 128], BF16, kind="ExternalInput").ap()
    out_e = nc.dram_tensor("out_e", [D, TPC], F32, kind="ExternalOutput").ap()
    ctx_h = nc.dram_tensor("ctx_h", [D, TPC], BF16)

    st_ = contextlib.ExitStack()
    E = st_.enter_context

    ones_b = E(nc.sbuf_tensor("ones_b", [128, 1], BF16))
    ones_f = E(nc.sbuf_tensor("ones_f", [1, 128], F32))
    ones_fb = E(nc.sbuf_tensor("ones_fb", [1, 128], BF16))
    bq_sb = E(nc.sbuf_tensor("bq_sb", [128, 32], F32))
    bk_sb = E(nc.sbuf_tensor("bk_sb", [128, 8], F32))
    bv_sb = E(nc.sbuf_tensor("bv_sb", [128, 8], F32))
    bd_sb = E(nc.sbuf_tensor("bd_sb", [128, 32], F32))
    bf_sb = E(nc.sbuf_tensor("bf_sb", [128, 32], F32))
    sel_sb = E(nc.sbuf_tensor("sel_sb", [8, 128], F32))
    lw_sb = E(nc.sbuf_tensor("lw_sb", [8, 1], F32))
    coef_sb = E(nc.sbuf_tensor("coef_sb", [128, 1], F32))

    LD = Sem(st_, nc, "LD")
    GP = Sem(st_, nc, "GP")
    ST = Sem(st_, nc, "ST")
    PE = Sem(st_, nc, "PE")
    ACT = Sem(st_, nc, "ACT")
    DVE = Sem(st_, nc, "DVE")

    block = E(nc.Block())
    rec = {}

    def slab1(s, dst_flat, src2d, col0):
        LD.inc(s.dma_start(
            out=dst_flat[:, :].rearrange("p (c o) -> p c o", c=32),
            in_=src2d[:, col0:col0 + 128].rearrange("(c p) o -> p c o", p=128)), 16)

    def p0(s):
        for dst, src in ((ones_b, onesbI), (ones_f, onesfI), (ones_fb, onesfbI),
                         (bq_sb, bqI), (bk_sb, bkI), (bv_sb, bvI),
                         (bd_sb, bdI), (bf_sb, bfI), (sel_sb, selI), (lw_sb, lwI)):
            LD.inc(s.dma_start(out=dst[:, :], in_=src[:, :]), 16)
        rec["p0"] = LD.n
    block.sync(p0)

    kvs = contextlib.ExitStack()
    KT = kvs.enter_context(nc.sbuf_tensor("KT", [128, NKV * NK], BF16))
    Vtm = kvs.enter_context(nc.sbuf_tensor("Vtm", [128, NKT * NKV * 128], BF16))

    with contextlib.ExitStack() as p1s:
        E1 = p1s.enter_context
        cosk_sb = E1(nc.sbuf_tensor("cosk_sb", [128, NK], BF16))
        identb = E1(nc.sbuf_tensor("identb", [128, 128], BF16))
        sink_sb = E1(nc.sbuf_tensor("sink_sb", [128, NK], BF16))
        kvw = E1(nc.sbuf_tensor("kvw", [128, 32 * 1024], BF16))
        xf = E1(nc.sbuf_tensor("xf", [128, 32 * 128], F32))
        xb = [E1(nc.sbuf_tensor(f"xb{i}", [128, 32 * 256], BF16)) for i in range(2)]
        vtmp = [E1(nc.sbuf_tensor(f"vtmp{i}", [128, 256], BF16)) for i in range(2)]
        ksw = E1(nc.sbuf_tensor("ksw", [128, NK], BF16))
        krt = E1(nc.sbuf_tensor("krt", [128, NK], BF16))
        ps_kv = [E1(nc.psum_tensor(f"ps_kv{i}", [128, 256], F32)) for i in range(2)]
        ps_vt = [E1(nc.psum_tensor(f"ps_vt{i}", [128, 256], BF16)) for i in range(2)]

        for phase in range(2):
            wsrc = wvT if phase == 0 else wkT
            tag = "v" if phase == 0 else "k"

            for i in range(8):
                def wld(s, i=i, phase=phase, wsrc=wsrc, tag=tag):
                    if phase == 0 and i == 0:
                        LD.inc(s.dma_start(out=cosk_sb[:, :], in_=coskI[:, :]), 16)
                        LD.inc(s.dma_start(out=sink_sb[:, :], in_=sinkI[:, :]), 16)
                        LD.inc(s.dma_start(out=identb[:, :], in_=identI[:, :]), 16)
                    if phase == 0 and i > 0:
                        s.wait_ge(ACT.h, rec[f"p1v_wc_{i-1}"])
                    if phase == 1:
                        s.wait_ge(ACT.h, rec["p1_xc_8_1"] if i == 0 else rec[f"p1k_wc_{i-1}"])
                    LD.inc(s.dma_start(
                        out=xf[:, :].rearrange("p (c o) -> p c o", c=4),
                        in_=wsrc[i * 512:(i + 1) * 512, :].rearrange("(c p) o -> p c o", p=128)), 16)
                    rec[f"p1{tag}_wl_{i}"] = LD.n
                block.sync(wld)

                def wcast(s, i=i, phase=phase, tag=tag):
                    s.wait_ge(LD.h, rec[f"p1{tag}_wl_{i}"])
                    if phase == 1 and i == 0:
                        s.wait_ge(PE.h, rec["p1v_pe_end"])
                    ACT.inc(s.activation(out=kvw[:, i * 4096:(i + 1) * 4096], in_=xf[:, :],
                                         func=AF.Identity, bias=0.0, scale=1.0))
                    rec[f"p1{tag}_wc_{i}"] = ACT.n
                block.scalar(wcast)

            NKU = NKT // 2
            for u in range(NKU):
                it = phase * NKU + u

                for half in range(2):
                    def xld(s, u=u, it=it, half=half, tag=tag):
                        if it == 0 and half == 0:
                            s.wait_ge(ACT.h, rec[f"p1{tag}_wc_7"])
                        if half == 0 and it > 0:
                            s.wait_ge(ACT.h, rec[f"p1_xc_{it-1}_1"])
                        if half == 1:
                            s.wait_ge(ACT.h, rec[f"p1_xc_{it}_0"])
                        if phase == 1 and it == NKU and half == 0:
                            s.wait_ge(ACT.h, rec["p1k_wc_7"])
                        tc = u * 2 + half
                        LD.inc(s.dma_start(
                            out=xf[:, :].rearrange("p (c o) -> p c o", c=32),
                            in_=xneed[:, tc * 128:(tc + 1) * 128]
                            .rearrange("(c p) o -> p c o", p=128)), 16)
                        rec[f"p1_xl_{it}_{half}"] = LD.n
                    block.sync(xld)

                    def xcast(s, it=it, half=half):
                        s.wait_ge(LD.h, rec[f"p1_xl_{it}_{half}"])
                        if half == 0 and it >= 2:
                            s.wait_ge(PE.h, rec[f"p1_mm_{it-2}"])
                        xbv = xb[it % 2][:, :].rearrange("p (c t) -> p c t", c=32)
                        ACT.inc(s.activation(out=xbv[:, :, half * 128:(half + 1) * 128],
                                             in_=xf[:, :].rearrange("p (c o) -> p c o", c=32),
                                             func=AF.Identity, bias=0.0, scale=1.0))
                        rec[f"p1_xc_{it}_{half}"] = ACT.n
                    block.scalar(xcast)

                for g in range(NKV):
                    gi = it * NKV + g

                    def mm1(t, g=g, gi=gi, it=it):
                        if g == 0:
                            t.wait_ge(ACT.h, rec[f"p1_xc_{it}_1"])
                        if gi >= 2:
                            t.wait_ge(DVE.h, rec[f"p1_ev_{gi-2}"])
                        for dch in range(32):
                            ins = t.matmul(
                                out=ps_kv[g % 2][:, :],
                                lhsT=kvw[:, dch * 1024 + g * 128: dch * 1024 + (g + 1) * 128],
                                rhs=xb[it % 2][:, dch * 256:(dch + 1) * 256],
                                start=(dch == 0), stop=(dch == 31))
                        PE.inc(ins)
                        rec[f"p1_smm_{gi}"] = PE.n
                    block.tensor(mm1)

                    if phase == 0:
                        def ev(v, g=g, gi=gi):
                            v.wait_ge(PE.h, rec[f"p1_smm_{gi}"])
                            if gi >= 2:
                                v.wait_ge(PE.h, rec[f"p1_vt_{gi-2}"])
                            DVE.inc(v.tensor_scalar(out=vtmp[g % 2][:, :], in0=ps_kv[g % 2][:, :],
                                                    scalar1=bv_sb[:, g:g + 1], scalar2=None,
                                                    op0=ALU.add))
                            rec[f"p1_ev_{gi}"] = DVE.n
                        block.vector(ev)

                        def vt(t, gi=gi):
                            pg = gi - 1
                            if pg < 0:
                                return
                            t.wait_ge(DVE.h, rec[f"p1_ev_{pg}"])
                            if pg >= 2:
                                t.wait_ge(DVE.h, rec[f"p1_vte_{pg-2}"])
                            for sub in range(2):
                                ins = t.transpose(out=ps_vt[pg % 2][:, sub * 128:(sub + 1) * 128],
                                                  in_=vtmp[pg % 2][:, sub * 128:(sub + 1) * 128],
                                                  identity=identb[:, :])
                            PE.inc(ins)
                            rec[f"p1_vt_{pg}"] = PE.n
                        block.tensor(vt)

                        def vte(v, gi=gi):
                            pg = gi - 1
                            if pg < 0:
                                return
                            u_p, g_p = pg // NKV, pg % NKV
                            v.wait_ge(PE.h, rec[f"p1_vt_{pg}"])
                            for sub in range(2):
                                tc = u_p * 2 + sub
                                ins = v.tensor_copy(
                                    out=Vtm[:, (tc * NKV + g_p) * 128:(tc * NKV + g_p + 1) * 128],
                                    in_=ps_vt[pg % 2][:, sub * 128:(sub + 1) * 128])
                            DVE.inc(ins)
                            rec[f"p1_vte_{pg}"] = DVE.n
                        block.vector(vte)
                    else:
                        def ev(v, g=g, gi=gi, u=u):
                            v.wait_ge(PE.h, rec[f"p1_smm_{gi}"])
                            DVE.inc(v.tensor_scalar(
                                out=KT[:, g * NK + u * 256: g * NK + (u + 1) * 256],
                                in0=ps_kv[g % 2][:, :],
                                scalar1=bk_sb[:, g:g + 1], scalar2=None, op0=ALU.add))
                            rec[f"p1_ev_{gi}"] = DVE.n
                        block.vector(ev)
                rec[f"p1_mm_{it}"] = PE.n
            if phase == 0:
                last = NKU * NKV - 1

                def vt_last(t, last=last):
                    t.wait_ge(DVE.h, rec[f"p1_ev_{last}"])
                    t.wait_ge(DVE.h, rec[f"p1_vte_{last-2}"])
                    for sub in range(2):
                        ins = t.transpose(out=ps_vt[last % 2][:, sub * 128:(sub + 1) * 128],
                                          in_=vtmp[last % 2][:, sub * 128:(sub + 1) * 128],
                                          identity=identb[:, :])
                    PE.inc(ins)
                    rec[f"p1_vt_{last}"] = PE.n
                block.tensor(vt_last)

                def vte_last(v, last=last):
                    u_p, g_p = last // NKV, last % NKV
                    v.wait_ge(PE.h, rec[f"p1_vt_{last}"])
                    for sub in range(2):
                        tc = u_p * 2 + sub
                        ins = v.tensor_copy(
                            out=Vtm[:, (tc * NKV + g_p) * 128:(tc * NKV + g_p + 1) * 128],
                            in_=ps_vt[last % 2][:, sub * 128:(sub + 1) * 128])
                    DVE.inc(ins)
                    rec[f"p1_vte_{last}"] = DVE.n
                block.vector(vte_last)
                rec["p1v_pe_end"] = PE.n

        rec["p1k_dve_end"] = DVE.n

        for g in range(NKV):
            def rsw(gq, g=g):
                gq.wait_ge(DVE.h, rec["p1k_dve_end"])
                if g > 0:
                    gq.wait_ge(DVE.h, rec[f"p1_rope_{g-1}"])
                sl = slice(g * NK, (g + 1) * NK)
                GP.inc(gq.dma_start(out=ksw[0:64, :], in_=KT[64:128, sl]), 16)
                GP.inc(gq.dma_start(out=ksw[64:128, :], in_=KT[0:64, sl]), 16)
                rec[f"p1_rsw_{g}"] = GP.n
            block.gpsimd(rsw)

            def rope(v, g=g):
                v.wait_ge(GP.h, rec[f"p1_rsw_{g}"])
                sl = slice(g * NK, (g + 1) * NK)
                v.tensor_tensor(out=krt[:, :], in0=KT[:, sl], in1=cosk_sb[:, :], op=ALU.mult)
                v.tensor_tensor(out=ksw[:, :], in0=ksw[:, :], in1=sink_sb[:, :], op=ALU.mult)
                DVE.inc(v.tensor_tensor(out=KT[:, sl], in0=krt[:, :], in1=ksw[:, :], op=ALU.add))
                rec[f"p1_rope_{g}"] = DVE.n
            block.vector(rope)
        rec["p1_dve_end"] = DVE.n
        rec["p1_pe_end"] = PE.n
        rec["p1_act_end"] = ACT.n
        rec["p1_gp_end"] = GP.n

    with contextlib.ExitStack() as p2s:
        E2 = p2s.enter_context
        xqb = E2(nc.sbuf_tensor("xqb", [128, 32 * TPC], BF16))
        cosq_sb = E2(nc.sbuf_tensor("cosq_sb", [128, TPC], BF16))
        sinq_sb = E2(nc.sbuf_tensor("sinq_sb", [128, TPC], BF16))
        mseg = E2(nc.sbuf_tensor("mseg", [128, NT1 * 512], BF16))
        qbuf = E2(nc.sbuf_tensor("qbuf", [128, 4 * TPC], BF16))
        wqf = E2(nc.sbuf_tensor("wqf", [128, 2048], F32))
        wqb = E2(nc.sbuf_tensor("wqb", [128, 4096], BF16))
        qsw = E2(nc.sbuf_tensor("qsw", [128, 512], BF16))
        rtmp = E2(nc.sbuf_tensor("rtmp", [128, 512], BF16))
        probs = [E2(nc.sbuf_tensor(f"probs{p}", [128, 1024], BF16)) for p in range(3)]
        ctxe = E2(nc.sbuf_tensor("ctxe", [128, 1024], BF16))
        lntmp = E2(nc.sbuf_tensor("lntmp", [1, 2 * 1024], F32))
        invl = E2(nc.sbuf_tensor("invl", [1, 1024], BF16))
        ps_s = [E2(nc.psum_tensor(f"ps_s{p}", [128, 1024], F32)) for p in range(2)]
        ps_c = E2(nc.psum_tensor("ps_c", [128, 1024], F32))
        ps_l = E2(nc.psum_tensor("ps_l", [1, 1024], F32))

        for i in range(16):
            def xql(s, i=i):
                if i == 0:
                    s.wait_ge(PE.h, rec["p1_pe_end"])
                    s.wait_ge(DVE.h, rec["p1_dve_end"])
                    s.wait_ge(GP.h, rec["p1_gp_end"])
                    LD.inc(s.dma_start(out=cosq_sb[:, :], in_=cosqI[:, :]), 16)
                    LD.inc(s.dma_start(out=sinq_sb[:, :], in_=sinqI[:, :]), 16)
                else:
                    s.wait_ge(ACT.h, rec[f"p2_xqc_{i-1}"])
                LD.inc(s.dma_start(
                    out=wqf[:, :].rearrange("p (c t) -> p c t", c=2),
                    in_=xown[i * 256:(i + 1) * 256, :].rearrange("(c p) t -> p c t", p=128)), 16)
                rec[f"p2_xql_{i}"] = LD.n
            block.sync(xql)

            def xqc(s, i=i):
                s.wait_ge(LD.h, rec[f"p2_xql_{i}"])
                ACT.inc(s.activation(out=xqb[:, i * 2048:(i + 1) * 2048], in_=wqf[:, :],
                                     func=AF.Identity, bias=0.0, scale=1.0))
                rec[f"p2_xqc_{i}"] = ACT.n
            block.scalar(xqc)


        for g in range(NKV):
            for j in range(QPG):
                h = g * QPG + j

                def sl_(s, h=h):
                    a0 = ACT.n
                    s.wait_ge(ACT.h, rec[f"p2_wqc_{h-1}"] if h > 0 else rec["p2_xqc_15"])
                    LD.inc(s.dma_start(
                        out=wqf[:, :].rearrange("p (c o) -> p c o", c=16),
                        in_=wqT[0:2048, h * 128:(h + 1) * 128]
                        .rearrange("(c p) o -> p c o", p=128)), 16)
                    rec[f"p2_wql_{h}_0"] = LD.n
                    s.wait_ge(ACT.h, a0 + 1)
                    LD.inc(s.dma_start(
                        out=wqf[:, :].rearrange("p (c o) -> p c o", c=16),
                        in_=wqT[2048:4096, h * 128:(h + 1) * 128]
                        .rearrange("(c p) o -> p c o", p=128)), 16)
                    rec[f"p2_wql_{h}_1"] = LD.n
                block.sync(sl_)

                def wc(s, h=h):
                    s.wait_ge(LD.h, rec[f"p2_wql_{h}_0"])
                    if h > 0:
                        s.wait_ge(PE.h, rec[f"p2_qmm_{h-1}"])
                    ACT.inc(s.activation(out=wqb[:, 0:2048], in_=wqf[:, :],
                                         func=AF.Identity, bias=0.0, scale=1.0))
                    s.wait_ge(LD.h, rec[f"p2_wql_{h}_1"])
                    ACT.inc(s.activation(out=wqb[:, 2048:4096], in_=wqf[:, :],
                                         func=AF.Identity, bias=0.0, scale=1.0))
                    rec[f"p2_wqc_{h}"] = ACT.n
                block.scalar(wc)

                def qmm(t, h=h, j=j, g=g):
                    t.wait_ge(ACT.h, rec[f"p2_wqc_{h}"])
                    if g > 0 and j == 0:
                        t.wait_ge(ACT.h, rec[f"p3_exp_last_{2*g-1}_1"])
                        t.wait_ge(DVE.h, rec[f"p3_ctxs_{2*g-1}_1"])
                    for tq in range(2):
                        if j >= 2:
                            t.wait_ge(DVE.h, rec[f"p2_qev_{h-2}_{tq}"])
                        for dch in range(32):
                            ins = t.matmul(
                                out=ps_s[j % 2][:, tq * 512:(tq + 1) * 512],
                                lhsT=wqb[:, dch * 128:(dch + 1) * 128],
                                rhs=xqb[:, dch * TPC + tq * 512: dch * TPC + (tq + 1) * 512],
                                start=(dch == 0), stop=(dch == 31))
                        PE.inc(ins)
                        rec[f"p2_qsm_{h}_{tq}"] = PE.n
                    rec[f"p2_qmm_{h}"] = PE.n
                block.tensor(qmm)

                def qev(v, h=h, j=j):
                    for tq in range(2):
                        v.wait_ge(PE.h, rec[f"p2_qsm_{h}_{tq}"])
                        DVE.inc(v.tensor_scalar(
                            out=qbuf[:, tq * 2048 + j * 512: tq * 2048 + (j + 1) * 512],
                            in0=ps_s[j % 2][:, tq * 512:(tq + 1) * 512],
                            scalar1=bq_sb[:, h:h + 1], scalar2=None, op0=ALU.add))
                        rec[f"p2_qev_{h}_{tq}"] = DVE.n
                    rec[f"p2_qev_{h}"] = DVE.n
                block.vector(qev)

                for tq in range(2):
                    def qrs(gq, h=h, j=j, tq=tq):
                        gq.wait_ge(DVE.h, rec[f"p2_qev_{h}_{tq}"])
                        if h > 0 or tq > 0:
                            gq.wait_ge(DVE.h, rec[f"p2_rope_{h}_{tq-1}" if tq > 0
                                                  else f"p2_rope_{h-1}_1"])
                        sl2 = slice(tq * 2048 + j * 512, tq * 2048 + (j + 1) * 512)
                        GP.inc(gq.dma_start(out=qsw[0:64, :], in_=qbuf[64:128, sl2]), 16)
                        GP.inc(gq.dma_start(out=qsw[64:128, :], in_=qbuf[0:64, sl2]), 16)
                        rec[f"p2_qrs_{h}_{tq}"] = GP.n
                    block.gpsimd(qrs)

                    def qrope(v, h=h, j=j, tq=tq):
                        v.wait_ge(GP.h, rec[f"p2_qrs_{h}_{tq}"])
                        sl2 = slice(tq * 2048 + j * 512, tq * 2048 + (j + 1) * 512)
                        cs = slice(tq * 512, (tq + 1) * 512)
                        v.tensor_tensor(out=rtmp[:, :], in0=qbuf[:, sl2], in1=cosq_sb[:, cs], op=ALU.mult)
                        v.tensor_tensor(out=qsw[:, :], in0=qsw[:, :], in1=sinq_sb[:, cs], op=ALU.mult)
                        DVE.inc(v.tensor_tensor(out=qbuf[:, sl2], in0=rtmp[:, :], in1=qsw[:, :], op=ALU.add))
                        rec[f"p2_rope_{h}_{tq}"] = DVE.n
                    block.vector(qrope)
                rec[f"p2_rope_{h}"] = DVE.n

            for hp in range(2):
                bid = g * 2 + hp
                for tq in range(2):
                    js = SCHED[tq]
                    n = len(js)
                    seg = f"{bid}_{tq}"
                    prev = (f"{bid}_{tq-1}" if tq > 0
                            else (f"{bid-1}_1" if bid > 0 else None))
                    moff0 = 0 if tq == 0 else NT0

                    inline_bc = (hp == 1 and tq == 1)

                    def emit_bc(t, pseg):
                        t.wait_ge(ACT.h, rec[f"p3_invl_{pseg}"])
                        t.wait_ge(DVE.h, rec[f"p3_ctxc_{pseg}"])
                        for hh in range(2):
                            ins = t.matmul(out=ps_c[:, hh * 512:(hh + 1) * 512],
                                           lhsT=ones_fb[:, :],
                                           rhs=invl[0:1, hh * 512:(hh + 1) * 512],
                                           start=True, stop=True)
                        PE.inc(ins)
                        assert PE.n == rec[f"p3_bc_{pseg}"]

                    def tsec(t, g=g, hp=hp, tq=tq, js=js, n=n, seg=seg, prev=prev,
                             inline_bc=inline_bc):
                        act0 = ACT.n
                        dve0 = DVE.n
                        rec[f"p3_act0_{seg}"] = act0
                        rec[f"p3_dve0_{seg}"] = dve0
                        t.wait_ge(DVE.h, rec[f"p2_rope_{g * QPG + 3}"])
                        first_wait = (rec[f"p3_exp_last_{prev}"] if prev is not None else act0)
                        for step in range(n + 2):
                            if step < n:
                                ji, j = step, js[step]
                                t.wait_ge(ACT.h, act0 + (ji - 1) if ji >= 2 else first_wait)
                                for hh in range(2):
                                    ins = t.matmul(
                                        out=ps_s[ji % 2][:, hh * 512:(hh + 1) * 512],
                                        lhsT=KT[:, g * NK + j * 128: g * NK + (j + 1) * 128],
                                        rhs=qbuf[:, tq * 2048 + (2 * hp + hh) * 512:
                                                 tq * 2048 + (2 * hp + hh + 1) * 512],
                                        start=True, stop=True)
                                PE.inc(ins)
                                rec[f"p3_sc_{seg}_{ji}"] = PE.n
                            if step == 2 and rec.get("p3_pending_bc") is not None:
                                emit_bc(t, rec.pop("p3_pending_bc"))
                            if step >= 2:
                                ji, j = step - 2, js[step - 2]
                                t.wait_ge(DVE.h, dve0 + ji + 1)
                                first, last_ = (ji == 0), (ji == n - 1)
                                for hh in range(2):
                                    t.matmul(out=ps_l[0:1, hh * 512:(hh + 1) * 512],
                                             lhsT=ones_b[:, :],
                                             rhs=probs[ji % 3][:, hh * 512:(hh + 1) * 512],
                                             start=first, stop=last_)
                                for hh in range(2):
                                    ins = t.matmul(
                                        out=ps_c[:, hh * 512:(hh + 1) * 512],
                                        lhsT=Vtm[:, (j * NKV + g) * 128:(j * NKV + g + 1) * 128],
                                        rhs=probs[ji % 3][:, hh * 512:(hh + 1) * 512],
                                        start=first, stop=last_)
                                PE.inc(ins)
                                rec[f"p3_cx_{seg}_{ji}"] = PE.n
                        rec[f"p3_mm_last_{seg}"] = PE.n
                        if inline_bc:
                            t.wait_ge(ACT.h, act0 + n + 2)
                            t.wait_ge(DVE.h, dve0 + n + 2)
                            for hh in range(2):
                                ins = t.matmul(out=ps_c[:, hh * 512:(hh + 1) * 512],
                                               lhsT=ones_fb[:, :],
                                               rhs=invl[0:1, hh * 512:(hh + 1) * 512],
                                               start=True, stop=True)
                            PE.inc(ins)
                            rec[f"p3_bc_{seg}"] = PE.n
                        else:
                            rec[f"p3_bc_{seg}"] = PE.n + 4
                            rec["p3_pending_bc"] = seg
                    block.tensor(tsec)

                    def mload(s, seg=seg, prev=prev, n=n, moff0=moff0):
                        if prev is not None:
                            s.wait_ge(DVE.h, rec[f"p3_mskend_{prev}"])
                        LD.inc(s.dma_start(out=mseg[:, 0:n * 512],
                                           in_=masksI[:, moff0 * 512:(moff0 + n) * 512]), 16)
                        rec[f"p3_ml_{seg}"] = LD.n
                    block.sync(mload)

                    def ssec(s, seg=seg, n=n):
                        act0 = rec[f"p3_act0_{seg}"]
                        assert ACT.n == act0
                        for ji in range(n):
                            s.wait_ge(PE.h, rec[f"p3_sc_{seg}_{ji}"])
                            if ji >= 3:
                                s.wait_ge(PE.h, rec[f"p3_cx_{seg}_{ji-3}"])
                            ACT.inc(s.activation(out=probs[ji % 3][:, :],
                                                 in_=ps_s[ji % 2][:, :],
                                                 func=AF.Exp, bias=0.0, scale=SM))
                            rec[f"p3_exp_{seg}_{ji}"] = ACT.n
                        rec[f"p3_exp_last_{seg}"] = ACT.n
                        s.wait_ge(PE.h, rec[f"p3_mm_last_{seg}"])
                        ACT.inc(s.activation(out=lntmp[0:1, 0:1024], in_=ps_l[:, :],
                                             func=AF.Ln, bias=0.0, scale=1.0))
                        rec[f"p3_lnl_{seg}"] = ACT.n
                        s.wait_ge(DVE.h, rec[f"p3_dve0_{seg}"] + n + 1)
                        ACT.inc(s.activation(out=invl[:, :], in_=lntmp[0:1, 1024:2048],
                                             func=AF.Exp, bias=0.0, scale=-1.0))
                        rec[f"p3_invl_{seg}"] = ACT.n
                        assert ACT.n == act0 + n + 2
                    block.scalar(ssec)

                    def vsec(v, seg=seg, prev=prev, n=n, moff0=moff0):
                        dve0 = rec[f"p3_dve0_{seg}"]
                        assert DVE.n == dve0
                        v.wait_ge(LD.h, rec[f"p3_ml_{seg}"])
                        for ji in range(n):
                            v.wait_ge(ACT.h, rec[f"p3_exp_{seg}_{ji}"])
                            for hh in range(2):
                                ins = v.tensor_tensor(
                                    out=probs[ji % 3][:, hh * 512:(hh + 1) * 512],
                                    in0=probs[ji % 3][:, hh * 512:(hh + 1) * 512],
                                    in1=mseg[:, ji * 512:(ji + 1) * 512],
                                    op=ALU.mult)
                            DVE.inc(ins)
                        rec[f"p3_mskend_{seg}"] = DVE.n
                        v.wait_ge(ACT.h, rec[f"p3_lnl_{seg}"])
                        DVE.inc(v.tensor_copy(out=lntmp[0:1, 1024:2048], in_=lntmp[0:1, 0:1024]))
                        v.wait_ge(PE.h, rec[f"p3_mm_last_{seg}"])
                        if prev is not None:
                            v.wait_ge(GP.h, rec[f"p3_st_{prev}"])
                        DVE.inc(v.tensor_copy(out=ctxe[:, :], in_=ps_c[:, :]))
                        rec[f"p3_ctxc_{seg}"] = DVE.n
                        assert DVE.n == dve0 + n + 2
                        v.wait_ge(PE.h, rec[f"p3_bc_{seg}"])
                        DVE.inc(v.tensor_tensor(out=ctxe[:, :], in0=ctxe[:, :],
                                                in1=ps_c[:, :], op=ALU.mult))
                        rec[f"p3_ctxs_{seg}"] = DVE.n
                    block.vector(vsec)

                    def gsec(gq, g=g, hp=hp, tq=tq, seg=seg):
                        gq.wait_ge(DVE.h, rec[f"p3_ctxs_{seg}"])
                        for hh in range(2):
                            h = g * QPG + 2 * hp + hh
                            GP.inc(gq.dma_start(
                                out=ctx_h.ap()[h * 128:(h + 1) * 128, tq * 512:(tq + 1) * 512],
                                in_=ctxe[:, hh * 512:(hh + 1) * 512]), 16)
                        rec[f"p3_st_{seg}"] = GP.n
                    block.gpsimd(gsec)
        rec["p3_pe_end"] = PE.n
        rec["p3_dve_end"] = DVE.n
        rec["p3_act_end"] = ACT.n
        rec["p3_gp_end"] = GP.n
    kvs.close()

    with contextlib.ExitStack() as p45:
        E4 = p45.enter_context
        y1 = E4(nc.sbuf_tensor("y1", [128, 32 * TPC], BF16))
        wdf = E4(nc.sbuf_tensor("wdf", [128, 4096], F32))
        wdb = [E4(nc.sbuf_tensor(f"wdb{i}", [128, 4096], BF16)) for i in range(2)]
        ysq = [E4(nc.sbuf_tensor(f"ysq{i}", [128, TPC], BF16)) for i in range(2)]
        tmpf = E4(nc.sbuf_tensor("tmpf", [128, TPC], F32))
        stat = E4(nc.sbuf_tensor("stat", [1, 3 * TPC], F32))

        def m_(i):
            return stat[0:1, i * TPC:(i + 1) * TPC]

        state = {}

        def dense_ffn(pfx, ctxbuf, ybuf, use_xres, bias_sb, wsrc, lora):
            with contextlib.ExitStack() as ls:
                ps_st = [ls.enter_context(nc.psum_tensor(f"ps_st_{pfx}{i}", [1, TPC], F32))
                         for i in range(2)]
                if lora:
                    with nc.psum_tensor("ps_h", [128, TPC], F32) as ps_h:
                        def hmm(t):
                            t.wait_ge(DVE.h, rec["p5_lprep"])
                            for half in range(2):
                                for dch in range(32):
                                    ins = t.matmul(
                                        out=ps_h[:, half * 512:(half + 1) * 512],
                                        lhsT=state["ATb"][:, dch * 128:(dch + 1) * 128],
                                        rhs=y1[:, dch * TPC + half * 512:
                                               dch * TPC + (half + 1) * 512],
                                        start=(dch == 0), stop=(dch == 31))
                            PE.inc(ins)
                            rec["p5_hmm"] = PE.n
                        block.tensor(hmm)

                        def hev(v):
                            v.wait_ge(PE.h, rec["p5_hmm"])
                            DVE.inc(v.tensor_scalar(out=state["hb"][:, :], in0=ps_h[:, :],
                                                    scalar1=coef_sb[:, 0:1], scalar2=None,
                                                    op0=ALU.mult))
                            rec["p5_h"] = DVE.n
                        block.vector(hev)

                with contextlib.ExitStack() as ds:
                    ps_d = [ds.enter_context(nc.psum_tensor(f"ps_d_{pfx}{i}", [128, TPC], F32))
                            for i in range(2)]
                    for og in range(32):
                        def wl(s, og=og):
                            if og == 0 and pfx == "p5":
                                s.wait_ge(DVE.h, rec["p5_lprep"])
                            if og >= 1:
                                s.wait_ge(ACT.h, rec[f"{pfx}_wc_{og-1}"])
                            slab1(s, wdf, wsrc, og * 128)
                            rec[f"{pfx}_wl_{og}"] = LD.n
                            if use_xres:
                                if og >= 2:
                                    s.wait_ge(DVE.h, rec[f"{pfx}_stt_{og-2}"])
                                LD.inc(s.dma_start(out=xres[og % 2][:, :],
                                                   in_=xown[og * 128:(og + 1) * 128, :]), 16)
                            rec[f"{pfx}_xl_{og}"] = LD.n
                        block.sync(wl)

                        def wc(s, og=og):
                            s.wait_ge(LD.h, rec[f"{pfx}_wl_{og}"])
                            if og >= 2:
                                s.wait_ge(PE.h, rec[f"{pfx}_mm_{og-2}"])
                            ACT.inc(s.activation(out=wdb[og % 2][:, :], in_=wdf[:, :],
                                                 func=AF.Identity, bias=0.0, scale=1.0))
                            rec[f"{pfx}_wc_{og}"] = ACT.n
                        block.scalar(wc)

                        def mm(t, og=og):
                            t.wait_ge(ACT.h, rec[f"{pfx}_wc_{og}"])
                            if og == 0:
                                if pfx == "p4":
                                    t.wait_ge(LD.h, rec["p4_ctx_ready"])
                                else:
                                    t.wait_ge(DVE.h, rec["p5_h"])
                            if og >= 2:
                                t.wait_ge(DVE.h, rec[f"{pfx}_stt_{og-2}"])
                            for half in range(2):
                                for dch in range(32):
                                    ins = t.matmul(
                                        out=ps_d[og % 2][:, half * 512:(half + 1) * 512],
                                        lhsT=wdb[og % 2][:, dch * 128:(dch + 1) * 128],
                                        rhs=ctxbuf[:, dch * TPC + half * 512:
                                                   dch * TPC + (half + 1) * 512],
                                        start=(dch == 0), stop=(dch == 31 and not lora))
                                if lora:
                                    ins = t.matmul(
                                        out=ps_d[og % 2][:, half * 512:(half + 1) * 512],
                                        lhsT=state["Bsc"][:, og * 128:(og + 1) * 128],
                                        rhs=state["hb"][:, half * 512:(half + 1) * 512],
                                        start=False, stop=True)
                            PE.inc(ins)
                            rec[f"{pfx}_mm_{og}"] = PE.n
                        block.tensor(mm)

                        def stt(v, og=og):
                            v.wait_ge(PE.h, rec[f"{pfx}_mm_{og}"])
                            if use_xres:
                                v.wait_ge(LD.h, rec[f"{pfx}_xl_{og}"])
                                res_ap = xres[og % 2][:, :]
                            else:
                                res_ap = y1[:, og * TPC:(og + 1) * TPC]
                            v.scalar_tensor_tensor(
                                out=ybuf[:, og * TPC:(og + 1) * TPC],
                                in0=ps_d[og % 2][:, :],
                                scalar=bias_sb[:, og:og + 1], op0=ALU.add,
                                in1=res_ap, op1=ALU.add)
                            DVE.inc(v.tensor_tensor(out=ysq[og % 2][:, :],
                                                    in0=ybuf[:, og * TPC:(og + 1) * TPC],
                                                    in1=ybuf[:, og * TPC:(og + 1) * TPC],
                                                    op=ALU.mult))
                            rec[f"{pfx}_stt_{og}"] = DVE.n
                        block.vector(stt)

                        def stm(t, og=og):
                            t.wait_ge(DVE.h, rec[f"{pfx}_stt_{og}"])
                            for half in range(2):
                                t.matmul(out=ps_st[0][0:1, half * 512:(half + 1) * 512],
                                         lhsT=ones_b[:, :],
                                         rhs=ybuf[:, og * TPC + half * 512:
                                                  og * TPC + (half + 1) * 512],
                                         start=(og == 0), stop=(og == 31))
                                ins = t.matmul(out=ps_st[1][0:1, half * 512:(half + 1) * 512],
                                               lhsT=ones_b[:, :],
                                               rhs=ysq[og % 2][:, half * 512:(half + 1) * 512],
                                               start=(og == 0), stop=(og == 31))
                            PE.inc(ins)
                            rec[f"{pfx}_st_{og}"] = PE.n
                        block.tensor(stm)

                def fin_v(v):
                    v.wait_ge(PE.h, rec[f"{pfx}_st_31"])
                    v.tensor_scalar(out=m_(0), in0=ps_st[0][:, :], scalar1=1.0 / D,
                                    scalar2=None, op0=ALU.mult)
                    v.tensor_scalar(out=m_(1), in0=ps_st[1][:, :], scalar1=1.0 / D,
                                    scalar2=EPS, op0=ALU.mult, op1=ALU.add)
                    v.tensor_tensor(out=m_(2), in0=m_(0), in1=m_(0), op=ALU.mult)
                    DVE.inc(v.tensor_tensor(out=m_(1), in0=m_(1), in1=m_(2), op=ALU.subtract))
                    rec[f"{pfx}_var"] = DVE.n
                block.vector(fin_v)

                def fin_s(s):
                    s.wait_ge(DVE.h, rec[f"{pfx}_var"])
                    ACT.inc(s.activation(out=m_(2), in_=m_(1), func=AF.Ln,
                                         bias=0.0, scale=1.0))
                    rec[f"{pfx}_ln"] = ACT.n
                block.scalar(fin_s)

                def fin_v2(v):
                    v.wait_ge(ACT.h, rec[f"{pfx}_ln"])
                    DVE.inc(v.tensor_copy(out=m_(1), in_=m_(2)))
                    rec[f"{pfx}_lncp"] = DVE.n
                block.vector(fin_v2)

                def fin_s2(s):
                    s.wait_ge(DVE.h, rec[f"{pfx}_lncp"])
                    ACT.inc(s.activation(out=m_(2), in_=m_(1), func=AF.Exp,
                                         bias=0.0, scale=-0.5))
                    rec[f"{pfx}_rstd"] = ACT.n
                block.scalar(fin_s2)

                with contextlib.ExitStack() as bs:
                    ps_bc = [bs.enter_context(nc.psum_tensor(f"ps_bc_{pfx}{i}", [128, TPC], F32))
                             for i in range(2)]

                    def bc(t):
                        t.wait_ge(ACT.h, rec[f"{pfx}_rstd"])
                        for half in range(2):
                            t.matmul(out=ps_bc[0][:, half * 512:(half + 1) * 512],
                                     lhsT=ones_f[:, :],
                                     rhs=m_(0)[0:1, half * 512:(half + 1) * 512],
                                     start=True, stop=True)
                            ins = t.matmul(out=ps_bc[1][:, half * 512:(half + 1) * 512],
                                           lhsT=ones_f[:, :],
                                           rhs=m_(2)[0:1, half * 512:(half + 1) * 512],
                                           start=True, stop=True)
                        PE.inc(ins)
                        rec[f"{pfx}_bc"] = PE.n
                    block.tensor(bc)

                    if pfx == "p4":
                        for og in range(32):
                            def nrm(v, og=og):
                                if og == 0:
                                    v.wait_ge(PE.h, rec["p4_bc"])
                                v.tensor_tensor(out=tmpf[:, :],
                                                in0=y1[:, og * TPC:(og + 1) * TPC],
                                                in1=ps_bc[0][:, :], op=ALU.subtract)
                                DVE.inc(v.tensor_tensor(out=y1[:, og * TPC:(og + 1) * TPC],
                                                        in0=tmpf[:, :],
                                                        in1=ps_bc[1][:, :], op=ALU.mult))
                                rec[f"p4_nrm_{og}"] = DVE.n
                            block.vector(nrm)
                        rec["p4_done"] = DVE.n
                    else:
                        for og in range(32):
                            def nrm(v, og=og):
                                if og == 0:
                                    v.wait_ge(PE.h, rec["p5_bc"])
                                if og >= 2:
                                    v.wait_ge(ST.h, rec[f"p5_out_{og-2}"])
                                v.tensor_tensor(out=state["outf"][og % 2][:, :],
                                                in0=ybuf[:, og * TPC:(og + 1) * TPC],
                                                in1=ps_bc[0][:, :], op=ALU.subtract)
                                DVE.inc(v.tensor_tensor(out=state["outf"][og % 2][:, :],
                                                        in0=state["outf"][og % 2][:, :],
                                                        in1=ps_bc[1][:, :], op=ALU.mult))
                                rec[f"p5_nrm_{og}"] = DVE.n
                            block.vector(nrm)

                            def ost(s, og=og):
                                s.wait_ge(DVE.h, rec[f"p5_nrm_{og}"])
                                ST.inc(s.dma_start(out=out_e[og * 128:(og + 1) * 128, :],
                                                   in_=state["outf"][og % 2][:, :]), 16)
                                rec[f"p5_out_{og}"] = ST.n
                            block.sync(ost)

        with contextlib.ExitStack() as cs:
            ctxb = cs.enter_context(nc.sbuf_tensor("ctxb", [128, 32 * TPC], BF16))
            xres = [cs.enter_context(nc.sbuf_tensor(f"xres{i}", [128, TPC], F32))
                    for i in range(2)]

            def ctxld(s):
                s.wait_ge(GP.h, rec["p3_gp_end"])
                s.wait_ge(DVE.h, rec["p3_dve_end"])
                s.wait_ge(PE.h, rec["p3_pe_end"])
                s.wait_ge(ACT.h, rec["p3_act_end"])
                for hh in range(32):
                    LD.inc(s.dma_start(out=ctxb[:, hh * TPC:(hh + 1) * TPC],
                                       in_=ctx_h.ap()[hh * 128:(hh + 1) * 128, :]), 16)
                rec["p4_ctx_ready"] = LD.n
            block.sync(ctxld)

            dense_ffn("p4", ctxb, y1, True, bd_sb, wdT, False)

        with contextlib.ExitStack() as fs:
            y2 = fs.enter_context(nc.sbuf_tensor("y2", [128, 32 * TPC], BF16))
            state["hb"] = fs.enter_context(nc.sbuf_tensor("hb", [128, TPC], BF16))
            state["ATb"] = fs.enter_context(nc.sbuf_tensor("ATb", [128, 32 * 128], BF16))
            state["Bsc"] = fs.enter_context(nc.sbuf_tensor("Bsc", [128, 4096], BF16))
            state["outf"] = [fs.enter_context(nc.sbuf_tensor(f"outf{i}", [128, TPC], F32))
                             for i in range(2)]

            def lprep_ld(s):
                s.wait_ge(ACT.h, rec["p4_wc_31"])
                s.wait_ge(DVE.h, rec["p4_done"])
                LD.inc(s.dma_start(
                    out=wdf[:, :].rearrange("p (c o) -> p c o", c=32),
                    in_=ATi[:, :].rearrange("(c p) o -> p c o", p=128)), 16)
                rec["p5_at_ld"] = LD.n
            block.sync(lprep_ld)

            with nc.psum_tensor("ps_misc", [128, 1], F32) as ps_misc:
                def lprep_pe(t):
                    t.wait_ge(LD.h, rec["p0"])
                    PE.inc(t.matmul(out=ps_misc[:, :], lhsT=sel_sb[:, :], rhs=lw_sb[:, :],
                                    start=True, stop=True))
                    rec["p5_coef_mm"] = PE.n
                block.tensor(lprep_pe)

                def lprep_v(v):
                    v.wait_ge(PE.h, rec["p5_coef_mm"])
                    DVE.inc(v.tensor_scalar(out=coef_sb[:, :], in0=ps_misc[:, :],
                                            scalar1=2.0, scalar2=None, op0=ALU.mult))
                    v.wait_ge(LD.h, rec["p5_at_ld"])
                    DVE.inc(v.tensor_copy(out=state["ATb"][:, :], in_=wdf[:, :]))
                    rec["p5_atb"] = DVE.n
                block.vector(lprep_v)

                def lprep_ld2(s):
                    s.wait_ge(DVE.h, rec["p5_atb"])
                    LD.inc(s.dma_start(out=wdf[:, :], in_=Bti[:, :]), 16)
                    rec["p5_bt_ld"] = LD.n
                block.sync(lprep_ld2)

                def lprep_v2(v):
                    v.wait_ge(LD.h, rec["p5_bt_ld"])
                    DVE.inc(v.tensor_copy(out=state["Bsc"][:, :], in_=wdf[:, :]))
                    rec["p5_lprep"] = DVE.n
                block.vector(lprep_v2)

            dense_ffn("p5", y1, y2, False, bf_sb, WTt, True)

            def fin(gq):
                gq.wait_ge(ST.h, rec["p5_out_31"])
            block.gpsimd(fin)

    st_.close()
    return nc


_NC_CACHE = None


def kernel(L0_attn_hidden_states, L0_attn_w_qkv, L0_attn_b_qkv, L0_attn_w_dense,
           L0_attn_b_dense, L0_ffn_lora_A, L0_ffn_lora_B, L0_ffn_W, L0_ffn_bias,
           L0_ffn_lora_weights):
    global _NC_CACHE
    x = np.asarray(L0_attn_hidden_states, dtype=np.float32)
    w_qkv = np.asarray(L0_attn_w_qkv, dtype=np.float32)
    b_qkv = np.asarray(L0_attn_b_qkv, dtype=np.float32)
    w_dense = np.asarray(L0_attn_w_dense, dtype=np.float32)
    b_dense = np.asarray(L0_attn_b_dense, dtype=np.float32)
    lora_A = np.asarray(L0_ffn_lora_A, dtype=np.float32)
    lora_B = np.asarray(L0_ffn_lora_B, dtype=np.float32)
    W = np.asarray(L0_ffn_W, dtype=np.float32)
    b_ffn = np.asarray(L0_ffn_bias, dtype=np.float32)
    lw = np.asarray(L0_ffn_lora_weights, dtype=np.float32)

    cos, sinS = rope_tables()
    wq_rows = np.concatenate([w_qkv[g * 768 + j * 128: g * 768 + (j + 1) * 128]
                              for g in range(NKV) for j in range(QPG)], axis=0)
    wk_rows = np.concatenate([w_qkv[g * 768 + 512: g * 768 + 640] for g in range(NKV)], axis=0)
    wv_rows = np.concatenate([w_qkv[g * 768 + 640: g * 768 + 768] for g in range(NKV)], axis=0)
    selm = np.zeros((8, 128), dtype=np.float32)
    for n in range(8):
        selm[n, n * 16:(n + 1) * 16] = 1.0
    common = dict(
        wqT=np.ascontiguousarray(wq_rows.T), wkT=np.ascontiguousarray(wk_rows.T),
        wvT=np.ascontiguousarray(wv_rows.T),
        wdT=np.ascontiguousarray(w_dense.T), WTt=np.ascontiguousarray(W.T),
        ATi=np.ascontiguousarray(lora_A.reshape(128, D).T),
        Bti=np.ascontiguousarray(lora_B.transpose(0, 2, 1).reshape(128, D)),
        lwI=np.ascontiguousarray(lw.reshape(8, 1)),
        bqI=np.ascontiguousarray(np.stack(
            [b_qkv[g * 768 + j * 128: g * 768 + (j + 1) * 128]
             for g in range(NKV) for j in range(QPG)], axis=1)),
        bkI=np.ascontiguousarray(np.stack(
            [b_qkv[g * 768 + 512: g * 768 + 640] for g in range(NKV)], axis=1)),
        bvI=np.ascontiguousarray(np.stack(
            [b_qkv[g * 768 + 640: g * 768 + 768] for g in range(NKV)], axis=1)),
        bdI=np.ascontiguousarray(b_dense.reshape(32, 128).T),
        bfI=np.ascontiguousarray(b_ffn.reshape(32, 128).T),
        onesbI=np.ones((128, 1), dtype=BF), onesfI=np.ones((1, 128), dtype=np.float32),
        onesfbI=np.ones((1, 128), dtype=BF),
        identI=np.eye(128, dtype=BF),
        selI=selm,
    )

    in_maps = []
    for c in range(NC_):
        b, q = c // 4, c % 4
        nb_pad, nreal, kpos = core_tables(q)
        xb_ = x[b]
        qoff = q * 1024
        m = dict(common)
        m["xneed"] = np.ascontiguousarray(xb_[kpos, :].T)
        m["xown"] = np.ascontiguousarray(xb_[qoff:qoff + 1024, :].T)
        m["cosqI"] = np.ascontiguousarray(cos[:, qoff:qoff + 1024].astype(BF))
        m["sinqI"] = np.ascontiguousarray(sinS[:, qoff:qoff + 1024].astype(BF))
        m["coskI"] = np.ascontiguousarray(cos[:, kpos].astype(BF))
        m["sinkI"] = np.ascontiguousarray(sinS[:, kpos].astype(BF))
        m["masksI"] = np.ascontiguousarray(mask_blob(q))
        in_maps.append(m)

    if _NC_CACHE is None:
        _NC_CACHE = build_nc()
    res = run_bass_kernel_spmd(_NC_CACHE, in_maps, core_ids=list(range(NC_)))
    outs = [np.asarray(res.results[c]["out_e"]).T for c in range(NC_)]
    return np.concatenate(outs, axis=0).reshape(B, S, D)


# version 10
# speedup vs baseline: 1.1857x; 1.1857x over previous
import contextlib
import numpy as np
import ml_dtypes
import concourse.bass as bass
import concourse.mybir as mybir
from concourse.bass_utils import run_bass_kernel_spmd

F32 = mybir.dt.float32
BF16 = mybir.dt.bfloat16
AF = mybir.ActivationFunctionType
ALU = mybir.AluOpType
BF = ml_dtypes.bfloat16

D = 4096
S = 4096
B = 2
TPC = 1024
HEAD = 128
NKV = 8
NQH = 32
QPG = 4
BLOCK = 64
LOCAL = 16
VSTRIDE = 8
SM = 1.0 / 128.0
EPS = 1e-5
NBLK = S // BLOCK
NB_PAD = 36
NK = NB_PAD * BLOCK
NKT = NK // 128
NC_ = 8


def allowed(qb, kb):
    return kb <= qb and (qb - kb < LOCAL or (kb + 1) % VSTRIDE == 0)


def needed_blocks(q):
    s = set()
    for qb in range(16 * q, 16 * q + 16):
        for kb in range(NBLK):
            if allowed(qb, kb):
                s.add(kb)
    return sorted(s)


def build_schedule():
    used = {0: set(), 1: set()}
    for q in range(4):
        nb = needed_blocks(q)
        for t in range(2):
            qb0 = 16 * q + 8 * t
            for j in range(NKT):
                for half in range(2):
                    cb = 2 * j + half
                    if cb >= len(nb):
                        continue
                    kb = nb[cb]
                    if any(allowed(qb, kb) for qb in range(qb0, qb0 + 8)):
                        used[t].add(j)
    return {t: sorted(used[t]) for t in range(2)}


SCHED = build_schedule()
NT0 = len(SCHED[0])
NT1 = len(SCHED[1])
NTILES = NT0 + NT1


def core_tables(q):
    nb = list(needed_blocks(q))
    nreal = len(nb)
    nb_pad = nb + [0] * (NB_PAD - nreal)
    kpos = np.zeros(NK, dtype=np.int64)
    for cb, kb in enumerate(nb_pad):
        kpos[cb * BLOCK:(cb + 1) * BLOCK] = kb * BLOCK + np.arange(BLOCK)
    return nb_pad, nreal, kpos


def mask_blob(q):
    nb_pad, nreal, kpos = core_tables(q)
    tiles = []
    for t in range(2):
        qp0 = q * 1024 + t * 512
        qpos = qp0 + np.arange(512)
        qb = qpos // BLOCK
        for j in SCHED[t]:
            m = np.zeros((128, 512), dtype=np.float32)
            for kk in range(128):
                ci = j * 128 + kk
                cb = ci // BLOCK
                if cb >= nreal:
                    continue
                kb = nb_pad[cb]
                kp = kpos[ci]
                ok = np.array([allowed(int(x), kb) for x in qb]) & (qpos >= kp)
                m[kk, :] = ok.astype(np.float32)
            tiles.append(m)
    return np.concatenate(tiles, axis=1).astype(BF)


def rope_tables():
    inv_freq = 1.0 / (1e6 ** (np.arange(0, HEAD, 2, dtype=np.float64) / HEAD))
    t = np.arange(S, dtype=np.float64)
    freqs = np.outer(t, inv_freq)
    emb = np.concatenate([freqs, freqs], axis=-1)
    cos = np.cos(emb).T.astype(np.float32)
    sin = np.sin(emb).T.astype(np.float32)
    sgn = np.ones((128, 1), dtype=np.float32)
    sgn[:64] = -1.0
    return cos, sin * sgn


class Sem:
    def __init__(self, st, nc, name):
        self.h = st.enter_context(nc.semaphore(name))
        self.n = 0

    def inc(self, instr, k=1):
        instr.then_inc(self.h, k)
        self.n += k
        return self.n


def build_nc():
    nc = bass.Bass("TRN2", target_bir_lowering=False)
    xneed = nc.dram_tensor("xneed", [D, NK], F32, kind="ExternalInput").ap()
    xown = nc.dram_tensor("xown", [D, TPC], F32, kind="ExternalInput").ap()
    wqT = nc.dram_tensor("wqT", [D, 4096], F32, kind="ExternalInput").ap()
    wkT = nc.dram_tensor("wkT", [D, 1024], F32, kind="ExternalInput").ap()
    wvT = nc.dram_tensor("wvT", [D, 1024], F32, kind="ExternalInput").ap()
    wdT = nc.dram_tensor("wdT", [D, 4096], F32, kind="ExternalInput").ap()
    WTt = nc.dram_tensor("WTt", [D, 4096], F32, kind="ExternalInput").ap()
    ATi = nc.dram_tensor("ATi", [D, 128], F32, kind="ExternalInput").ap()
    Bti = nc.dram_tensor("Bti", [128, 4096], F32, kind="ExternalInput").ap()
    lwI = nc.dram_tensor("lwI", [8, 1], F32, kind="ExternalInput").ap()
    selI = nc.dram_tensor("selI", [8, 128], F32, kind="ExternalInput").ap()
    bqI = nc.dram_tensor("bqI", [128, 32], F32, kind="ExternalInput").ap()
    bkI = nc.dram_tensor("bkI", [128, 8], F32, kind="ExternalInput").ap()
    bvI = nc.dram_tensor("bvI", [128, 8], F32, kind="ExternalInput").ap()
    bdI = nc.dram_tensor("bdI", [128, 32], F32, kind="ExternalInput").ap()
    bfI = nc.dram_tensor("bfI", [128, 32], F32, kind="ExternalInput").ap()
    cosqI = nc.dram_tensor("cosqI", [128, TPC], BF16, kind="ExternalInput").ap()
    sinqI = nc.dram_tensor("sinqI", [128, TPC], BF16, kind="ExternalInput").ap()
    coskI = nc.dram_tensor("coskI", [128, NK], BF16, kind="ExternalInput").ap()
    sinkI = nc.dram_tensor("sinkI", [128, NK], BF16, kind="ExternalInput").ap()
    masksI = nc.dram_tensor("masksI", [128, NTILES * 512], BF16, kind="ExternalInput").ap()
    onesbI = nc.dram_tensor("onesbI", [128, 1], BF16, kind="ExternalInput").ap()
    onesfI = nc.dram_tensor("onesfI", [1, 128], F32, kind="ExternalInput").ap()
    onesfbI = nc.dram_tensor("onesfbI", [1, 128], BF16, kind="ExternalInput").ap()
    identI = nc.dram_tensor("identI", [128, 128], BF16, kind="ExternalInput").ap()
    out_e = nc.dram_tensor("out_e", [D, TPC], F32, kind="ExternalOutput").ap()
    ctx_h = nc.dram_tensor("ctx_h", [D, TPC], BF16)

    st_ = contextlib.ExitStack()
    E = st_.enter_context

    ones_b = E(nc.sbuf_tensor("ones_b", [128, 1], BF16))
    ones_f = E(nc.sbuf_tensor("ones_f", [1, 128], F32))
    ones_fb = E(nc.sbuf_tensor("ones_fb", [1, 128], BF16))
    bq_sb = E(nc.sbuf_tensor("bq_sb", [128, 32], F32))
    bk_sb = E(nc.sbuf_tensor("bk_sb", [128, 8], F32))
    bv_sb = E(nc.sbuf_tensor("bv_sb", [128, 8], F32))
    bd_sb = E(nc.sbuf_tensor("bd_sb", [128, 32], F32))
    bf_sb = E(nc.sbuf_tensor("bf_sb", [128, 32], F32))
    sel_sb = E(nc.sbuf_tensor("sel_sb", [8, 128], F32))
    lw_sb = E(nc.sbuf_tensor("lw_sb", [8, 1], F32))
    coef_sb = E(nc.sbuf_tensor("coef_sb", [128, 1], F32))

    LD = Sem(st_, nc, "LD")
    GP = Sem(st_, nc, "GP")
    ST = Sem(st_, nc, "ST")
    PE = Sem(st_, nc, "PE")
    ACT = Sem(st_, nc, "ACT")
    DVE = Sem(st_, nc, "DVE")

    block = E(nc.Block())
    rec = {}

    def slab1(s, dst_flat, src2d, col0):
        LD.inc(s.dma_start(
            out=dst_flat[:, :].rearrange("p (c o) -> p c o", c=32),
            in_=src2d[:, col0:col0 + 128].rearrange("(c p) o -> p c o", p=128)), 16)

    def p0(s):
        for dst, src in ((ones_b, onesbI), (ones_f, onesfI), (ones_fb, onesfbI),
                         (bq_sb, bqI), (bk_sb, bkI), (bv_sb, bvI),
                         (bd_sb, bdI), (bf_sb, bfI), (sel_sb, selI), (lw_sb, lwI)):
            LD.inc(s.dma_start(out=dst[:, :], in_=src[:, :]), 16)
        rec["p0"] = LD.n
    block.sync(p0)

    kvs = contextlib.ExitStack()
    KT = kvs.enter_context(nc.sbuf_tensor("KT", [128, NKV * NK], BF16))
    Vtm = kvs.enter_context(nc.sbuf_tensor("Vtm", [128, NKT * NKV * 128], BF16))

    with contextlib.ExitStack() as p1s:
        E1 = p1s.enter_context
        cosk_sb = E1(nc.sbuf_tensor("cosk_sb", [128, NK], BF16))
        identb = E1(nc.sbuf_tensor("identb", [128, 128], BF16))
        sink_sb = E1(nc.sbuf_tensor("sink_sb", [128, NK], BF16))
        kvw = E1(nc.sbuf_tensor("kvw", [128, 32 * 1024], BF16))
        xf = E1(nc.sbuf_tensor("xf", [128, 32 * 128], F32))
        xb = [E1(nc.sbuf_tensor(f"xb{i}", [128, 32 * 256], BF16)) for i in range(2)]
        vtmp = [E1(nc.sbuf_tensor(f"vtmp{i}", [128, 256], BF16)) for i in range(2)]
        ksw = E1(nc.sbuf_tensor("ksw", [128, NK], BF16))
        krt = E1(nc.sbuf_tensor("krt", [128, NK], BF16))
        ps_kv = [E1(nc.psum_tensor(f"ps_kv{i}", [128, 256], F32)) for i in range(2)]
        ps_vt = [E1(nc.psum_tensor(f"ps_vt{i}", [128, 256], BF16)) for i in range(2)]

        for phase in range(2):
            wsrc = wvT if phase == 0 else wkT
            tag = "v" if phase == 0 else "k"

            for i in range(8):
                def wld(s, i=i, phase=phase, wsrc=wsrc, tag=tag):
                    if phase == 0 and i == 0:
                        LD.inc(s.dma_start(out=cosk_sb[:, :], in_=coskI[:, :]), 16)
                        LD.inc(s.dma_start(out=sink_sb[:, :], in_=sinkI[:, :]), 16)
                        LD.inc(s.dma_start(out=identb[:, :], in_=identI[:, :]), 16)
                    if phase == 0 and i > 0:
                        s.wait_ge(ACT.h, rec[f"p1v_wc_{i-1}"])
                    if phase == 1:
                        s.wait_ge(ACT.h, rec["p1_xc_8_1"] if i == 0 else rec[f"p1k_wc_{i-1}"])
                    LD.inc(s.dma_start(
                        out=xf[:, :].rearrange("p (c o) -> p c o", c=4),
                        in_=wsrc[i * 512:(i + 1) * 512, :].rearrange("(c p) o -> p c o", p=128)), 16)
                    rec[f"p1{tag}_wl_{i}"] = LD.n
                block.sync(wld)

                def wcast(s, i=i, phase=phase, tag=tag):
                    s.wait_ge(LD.h, rec[f"p1{tag}_wl_{i}"])
                    if phase == 1 and i == 0:
                        s.wait_ge(PE.h, rec["p1v_pe_end"])
                    ACT.inc(s.activation(out=kvw[:, i * 4096:(i + 1) * 4096], in_=xf[:, :],
                                         func=AF.Identity, bias=0.0, scale=1.0))
                    rec[f"p1{tag}_wc_{i}"] = ACT.n
                block.scalar(wcast)

            NKU = NKT // 2
            for u in range(NKU):
                it = phase * NKU + u

                for half in range(2):
                    def xld(s, u=u, it=it, half=half, tag=tag):
                        if it == 0 and half == 0:
                            s.wait_ge(ACT.h, rec[f"p1{tag}_wc_7"])
                        if half == 0 and it > 0:
                            s.wait_ge(ACT.h, rec[f"p1_xc_{it-1}_1"])
                        if half == 1:
                            s.wait_ge(ACT.h, rec[f"p1_xc_{it}_0"])
                        if phase == 1 and it == NKU and half == 0:
                            s.wait_ge(ACT.h, rec["p1k_wc_7"])
                        tc = u * 2 + half
                        LD.inc(s.dma_start(
                            out=xf[:, :].rearrange("p (c o) -> p c o", c=32),
                            in_=xneed[:, tc * 128:(tc + 1) * 128]
                            .rearrange("(c p) o -> p c o", p=128)), 16)
                        rec[f"p1_xl_{it}_{half}"] = LD.n
                    block.sync(xld)

                    def xcast(s, it=it, half=half):
                        s.wait_ge(LD.h, rec[f"p1_xl_{it}_{half}"])
                        if half == 0 and it >= 2:
                            s.wait_ge(PE.h, rec[f"p1_mm_{it-2}"])
                        xbv = xb[it % 2][:, :].rearrange("p (c t) -> p c t", c=32)
                        ACT.inc(s.activation(out=xbv[:, :, half * 128:(half + 1) * 128],
                                             in_=xf[:, :].rearrange("p (c o) -> p c o", c=32),
                                             func=AF.Identity, bias=0.0, scale=1.0))
                        rec[f"p1_xc_{it}_{half}"] = ACT.n
                    block.scalar(xcast)

                for g in range(NKV):
                    gi = it * NKV + g

                    def mm1(t, g=g, gi=gi, it=it):
                        if g == 0:
                            t.wait_ge(ACT.h, rec[f"p1_xc_{it}_1"])
                        if gi >= 2:
                            t.wait_ge(DVE.h, rec[f"p1_ev_{gi-2}"])
                        for dch in range(32):
                            ins = t.matmul(
                                out=ps_kv[g % 2][:, :],
                                lhsT=kvw[:, dch * 1024 + g * 128: dch * 1024 + (g + 1) * 128],
                                rhs=xb[it % 2][:, dch * 256:(dch + 1) * 256],
                                start=(dch == 0), stop=(dch == 31))
                        PE.inc(ins)
                        rec[f"p1_smm_{gi}"] = PE.n
                    block.tensor(mm1)

                    if phase == 0:
                        def ev(v, g=g, gi=gi):
                            v.wait_ge(PE.h, rec[f"p1_smm_{gi}"])
                            if gi >= 2:
                                v.wait_ge(PE.h, rec[f"p1_vt_{gi-2}"])
                            DVE.inc(v.tensor_scalar(out=vtmp[g % 2][:, :], in0=ps_kv[g % 2][:, :],
                                                    scalar1=bv_sb[:, g:g + 1], scalar2=None,
                                                    op0=ALU.add))
                            rec[f"p1_ev_{gi}"] = DVE.n
                        block.vector(ev)

                        def vt(t, gi=gi):
                            pg = gi - 1
                            if pg < 0:
                                return
                            t.wait_ge(DVE.h, rec[f"p1_ev_{pg}"])
                            if pg >= 2:
                                t.wait_ge(DVE.h, rec[f"p1_vte_{pg-2}"])
                            for sub in range(2):
                                ins = t.transpose(out=ps_vt[pg % 2][:, sub * 128:(sub + 1) * 128],
                                                  in_=vtmp[pg % 2][:, sub * 128:(sub + 1) * 128],
                                                  identity=identb[:, :])
                            PE.inc(ins)
                            rec[f"p1_vt_{pg}"] = PE.n
                        block.tensor(vt)

                        def vte(v, gi=gi):
                            pg = gi - 1
                            if pg < 0:
                                return
                            u_p, g_p = pg // NKV, pg % NKV
                            v.wait_ge(PE.h, rec[f"p1_vt_{pg}"])
                            for sub in range(2):
                                tc = u_p * 2 + sub
                                ins = v.tensor_copy(
                                    out=Vtm[:, (tc * NKV + g_p) * 128:(tc * NKV + g_p + 1) * 128],
                                    in_=ps_vt[pg % 2][:, sub * 128:(sub + 1) * 128])
                            DVE.inc(ins)
                            rec[f"p1_vte_{pg}"] = DVE.n
                        block.vector(vte)
                    else:
                        def ev(v, g=g, gi=gi, u=u):
                            v.wait_ge(PE.h, rec[f"p1_smm_{gi}"])
                            DVE.inc(v.tensor_scalar(
                                out=KT[:, g * NK + u * 256: g * NK + (u + 1) * 256],
                                in0=ps_kv[g % 2][:, :],
                                scalar1=bk_sb[:, g:g + 1], scalar2=None, op0=ALU.add))
                            rec[f"p1_ev_{gi}"] = DVE.n
                        block.vector(ev)
                rec[f"p1_mm_{it}"] = PE.n
            if phase == 0:
                last = NKU * NKV - 1

                def vt_last(t, last=last):
                    t.wait_ge(DVE.h, rec[f"p1_ev_{last}"])
                    t.wait_ge(DVE.h, rec[f"p1_vte_{last-2}"])
                    for sub in range(2):
                        ins = t.transpose(out=ps_vt[last % 2][:, sub * 128:(sub + 1) * 128],
                                          in_=vtmp[last % 2][:, sub * 128:(sub + 1) * 128],
                                          identity=identb[:, :])
                    PE.inc(ins)
                    rec[f"p1_vt_{last}"] = PE.n
                block.tensor(vt_last)

                def vte_last(v, last=last):
                    u_p, g_p = last // NKV, last % NKV
                    v.wait_ge(PE.h, rec[f"p1_vt_{last}"])
                    for sub in range(2):
                        tc = u_p * 2 + sub
                        ins = v.tensor_copy(
                            out=Vtm[:, (tc * NKV + g_p) * 128:(tc * NKV + g_p + 1) * 128],
                            in_=ps_vt[last % 2][:, sub * 128:(sub + 1) * 128])
                    DVE.inc(ins)
                    rec[f"p1_vte_{last}"] = DVE.n
                block.vector(vte_last)
                rec["p1v_pe_end"] = PE.n

        rec["p1k_dve_end"] = DVE.n

        for g in range(NKV):
            def rsw(gq, g=g):
                gq.wait_ge(DVE.h, rec["p1k_dve_end"])
                if g > 0:
                    gq.wait_ge(DVE.h, rec[f"p1_rope_{g-1}"])
                sl = slice(g * NK, (g + 1) * NK)
                GP.inc(gq.dma_start(out=ksw[0:64, :], in_=KT[64:128, sl]), 16)
                GP.inc(gq.dma_start(out=ksw[64:128, :], in_=KT[0:64, sl]), 16)
                rec[f"p1_rsw_{g}"] = GP.n
            block.gpsimd(rsw)

            def rope(v, g=g):
                v.wait_ge(GP.h, rec[f"p1_rsw_{g}"])
                sl = slice(g * NK, (g + 1) * NK)
                v.tensor_tensor(out=krt[:, :], in0=KT[:, sl], in1=cosk_sb[:, :], op=ALU.mult)
                v.tensor_tensor(out=ksw[:, :], in0=ksw[:, :], in1=sink_sb[:, :], op=ALU.mult)
                DVE.inc(v.tensor_tensor(out=KT[:, sl], in0=krt[:, :], in1=ksw[:, :], op=ALU.add))
                rec[f"p1_rope_{g}"] = DVE.n
            block.vector(rope)
        rec["p1_dve_end"] = DVE.n
        rec["p1_pe_end"] = PE.n
        rec["p1_act_end"] = ACT.n
        rec["p1_gp_end"] = GP.n

    with contextlib.ExitStack() as p2s:
        E2 = p2s.enter_context
        xqb = E2(nc.sbuf_tensor("xqb", [128, 32 * TPC], BF16))
        cosq_sb = E2(nc.sbuf_tensor("cosq_sb", [128, TPC], BF16))
        sinq_sb = E2(nc.sbuf_tensor("sinq_sb", [128, TPC], BF16))
        mseg = E2(nc.sbuf_tensor("mseg", [128, NT1 * 512], BF16))
        qbuf = E2(nc.sbuf_tensor("qbuf", [128, 4 * TPC], BF16))
        wqf = E2(nc.sbuf_tensor("wqf", [128, 2048], F32))
        wqb = E2(nc.sbuf_tensor("wqb", [128, 4096], BF16))
        qsw = E2(nc.sbuf_tensor("qsw", [128, 512], BF16))
        rtmp = E2(nc.sbuf_tensor("rtmp", [128, 512], BF16))
        probs = [E2(nc.sbuf_tensor(f"probs{p}", [128, 1024], BF16)) for p in range(3)]
        ctxe = E2(nc.sbuf_tensor("ctxe", [128, 1024], BF16))
        lntmp = E2(nc.sbuf_tensor("lntmp", [1, 2 * 1024], F32))
        invl = E2(nc.sbuf_tensor("invl", [1, 1024], BF16))
        ps_s = [E2(nc.psum_tensor(f"ps_s{p}", [128, 1024], F32)) for p in range(2)]
        ps_c = E2(nc.psum_tensor("ps_c", [128, 1024], F32))
        ps_l = E2(nc.psum_tensor("ps_l", [1, 1024], F32))

        for i in range(16):
            def xql(s, i=i):
                if i == 0:
                    s.wait_ge(PE.h, rec["p1_pe_end"])
                    s.wait_ge(DVE.h, rec["p1_dve_end"])
                    s.wait_ge(GP.h, rec["p1_gp_end"])
                    LD.inc(s.dma_start(out=cosq_sb[:, :], in_=cosqI[:, :]), 16)
                    LD.inc(s.dma_start(out=sinq_sb[:, :], in_=sinqI[:, :]), 16)
                else:
                    s.wait_ge(ACT.h, rec[f"p2_xqc_{i-1}"])
                LD.inc(s.dma_start(
                    out=wqf[:, :].rearrange("p (c t) -> p c t", c=2),
                    in_=xown[i * 256:(i + 1) * 256, :].rearrange("(c p) t -> p c t", p=128)), 16)
                rec[f"p2_xql_{i}"] = LD.n
            block.sync(xql)

            def xqc(s, i=i):
                s.wait_ge(LD.h, rec[f"p2_xql_{i}"])
                ACT.inc(s.activation(out=xqb[:, i * 2048:(i + 1) * 2048], in_=wqf[:, :],
                                     func=AF.Identity, bias=0.0, scale=1.0))
                rec[f"p2_xqc_{i}"] = ACT.n
            block.scalar(xqc)


        for g in range(NKV):
            for j in range(QPG):
                h = g * QPG + j

                def sl_(s, h=h):
                    a0 = ACT.n
                    s.wait_ge(ACT.h, rec[f"p2_wqc_{h-1}"] if h > 0 else rec["p2_xqc_15"])
                    LD.inc(s.dma_start(
                        out=wqf[:, :].rearrange("p (c o) -> p c o", c=16),
                        in_=wqT[0:2048, h * 128:(h + 1) * 128]
                        .rearrange("(c p) o -> p c o", p=128)), 16)
                    rec[f"p2_wql_{h}_0"] = LD.n
                    s.wait_ge(ACT.h, a0 + 1)
                    LD.inc(s.dma_start(
                        out=wqf[:, :].rearrange("p (c o) -> p c o", c=16),
                        in_=wqT[2048:4096, h * 128:(h + 1) * 128]
                        .rearrange("(c p) o -> p c o", p=128)), 16)
                    rec[f"p2_wql_{h}_1"] = LD.n
                block.sync(sl_)

                def wc(s, h=h):
                    s.wait_ge(LD.h, rec[f"p2_wql_{h}_0"])
                    if h > 0:
                        s.wait_ge(PE.h, rec[f"p2_qmid_{h-1}"])
                    ACT.inc(s.activation(out=wqb[:, 0:2048], in_=wqf[:, :],
                                         func=AF.Identity, bias=0.0, scale=1.0))
                    rec[f"p2_wqc0_{h}"] = ACT.n
                    s.wait_ge(LD.h, rec[f"p2_wql_{h}_1"])
                    if h > 0:
                        s.wait_ge(PE.h, rec[f"p2_qmm_{h-1}"])
                    ACT.inc(s.activation(out=wqb[:, 2048:4096], in_=wqf[:, :],
                                         func=AF.Identity, bias=0.0, scale=1.0))
                    rec[f"p2_wqc_{h}"] = ACT.n
                block.scalar(wc)

                def qmm(t, h=h, j=j, g=g):
                    t.wait_ge(ACT.h, rec[f"p2_wqc0_{h}"])
                    if g > 0 and j == 0:
                        t.wait_ge(ACT.h, rec[f"p3_exp_last_{2*g-1}_1"])
                        t.wait_ge(DVE.h, rec[f"p3_ctxs_{2*g-1}_1"])
                    if j >= 2:
                        t.wait_ge(DVE.h, rec[f"p2_qev_{h-2}_1"])
                    for dch in range(32):
                        if dch == 16:
                            t.wait_ge(ACT.h, rec[f"p2_wqc_{h}"])
                            rec[f"p2_qmid_{h}"] = PE.n
                        for tq in range(2):
                            ins = t.matmul(
                                out=ps_s[j % 2][:, tq * 512:(tq + 1) * 512],
                                lhsT=wqb[:, dch * 128:(dch + 1) * 128],
                                rhs=xqb[:, dch * TPC + tq * 512: dch * TPC + (tq + 1) * 512],
                                start=(dch == 0), stop=(dch == 31))
                            if dch == 15 and tq == 1:
                                PE.inc(ins)
                    PE.inc(ins)
                    rec[f"p2_qsm_{h}_0"] = PE.n
                    rec[f"p2_qsm_{h}_1"] = PE.n
                    rec[f"p2_qmm_{h}"] = PE.n
                block.tensor(qmm)

                def qev(v, h=h, j=j):
                    for tq in range(2):
                        v.wait_ge(PE.h, rec[f"p2_qsm_{h}_{tq}"])
                        DVE.inc(v.tensor_scalar(
                            out=qbuf[:, tq * 2048 + j * 512: tq * 2048 + (j + 1) * 512],
                            in0=ps_s[j % 2][:, tq * 512:(tq + 1) * 512],
                            scalar1=bq_sb[:, h:h + 1], scalar2=None, op0=ALU.add))
                        rec[f"p2_qev_{h}_{tq}"] = DVE.n
                    rec[f"p2_qev_{h}"] = DVE.n
                block.vector(qev)

                for tq in range(2):
                    def qrs(gq, h=h, j=j, tq=tq):
                        gq.wait_ge(DVE.h, rec[f"p2_qev_{h}_{tq}"])
                        if h > 0 or tq > 0:
                            gq.wait_ge(DVE.h, rec[f"p2_rope_{h}_{tq-1}" if tq > 0
                                                  else f"p2_rope_{h-1}_1"])
                        sl2 = slice(tq * 2048 + j * 512, tq * 2048 + (j + 1) * 512)
                        GP.inc(gq.dma_start(out=qsw[0:64, :], in_=qbuf[64:128, sl2]), 16)
                        GP.inc(gq.dma_start(out=qsw[64:128, :], in_=qbuf[0:64, sl2]), 16)
                        rec[f"p2_qrs_{h}_{tq}"] = GP.n
                    block.gpsimd(qrs)

                    def qrope(v, h=h, j=j, tq=tq):
                        v.wait_ge(GP.h, rec[f"p2_qrs_{h}_{tq}"])
                        sl2 = slice(tq * 2048 + j * 512, tq * 2048 + (j + 1) * 512)
                        cs = slice(tq * 512, (tq + 1) * 512)
                        v.tensor_tensor(out=rtmp[:, :], in0=qbuf[:, sl2], in1=cosq_sb[:, cs], op=ALU.mult)
                        v.tensor_tensor(out=qsw[:, :], in0=qsw[:, :], in1=sinq_sb[:, cs], op=ALU.mult)
                        DVE.inc(v.tensor_tensor(out=qbuf[:, sl2], in0=rtmp[:, :], in1=qsw[:, :], op=ALU.add))
                        rec[f"p2_rope_{h}_{tq}"] = DVE.n
                    block.vector(qrope)
                rec[f"p2_rope_{h}"] = DVE.n

            for hp in range(2):
                bid = g * 2 + hp
                for tq in range(2):
                    js = SCHED[tq]
                    n = len(js)
                    seg = f"{bid}_{tq}"
                    prev = (f"{bid}_{tq-1}" if tq > 0
                            else (f"{bid-1}_1" if bid > 0 else None))
                    moff0 = 0 if tq == 0 else NT0

                    inline_bc = (hp == 1 and tq == 1)

                    def emit_bc(t, pseg):
                        t.wait_ge(ACT.h, rec[f"p3_invl_{pseg}"])
                        t.wait_ge(DVE.h, rec[f"p3_ctxc_{pseg}"])
                        for hh in range(2):
                            ins = t.matmul(out=ps_c[:, hh * 512:(hh + 1) * 512],
                                           lhsT=ones_fb[:, :],
                                           rhs=invl[0:1, hh * 512:(hh + 1) * 512],
                                           start=True, stop=True)
                        PE.inc(ins)
                        assert PE.n == rec[f"p3_bc_{pseg}"]

                    def tsec(t, g=g, hp=hp, tq=tq, js=js, n=n, seg=seg, prev=prev,
                             inline_bc=inline_bc):
                        act0 = ACT.n
                        dve0 = DVE.n
                        rec[f"p3_act0_{seg}"] = act0
                        rec[f"p3_dve0_{seg}"] = dve0
                        t.wait_ge(DVE.h, rec[f"p2_rope_{g * QPG + 3}"])
                        first_wait = (rec[f"p3_exp_last_{prev}"] if prev is not None else act0)
                        for step in range(n + 2):
                            if step < n:
                                ji, j = step, js[step]
                                t.wait_ge(ACT.h, act0 + (ji - 1) if ji >= 2 else first_wait)
                                for hh in range(2):
                                    ins = t.matmul(
                                        out=ps_s[ji % 2][:, hh * 512:(hh + 1) * 512],
                                        lhsT=KT[:, g * NK + j * 128: g * NK + (j + 1) * 128],
                                        rhs=qbuf[:, tq * 2048 + (2 * hp + hh) * 512:
                                                 tq * 2048 + (2 * hp + hh + 1) * 512],
                                        start=True, stop=True)
                                PE.inc(ins)
                                rec[f"p3_sc_{seg}_{ji}"] = PE.n
                            if step == 2 and rec.get("p3_pending_bc") is not None:
                                emit_bc(t, rec.pop("p3_pending_bc"))
                            if step >= 2:
                                ji, j = step - 2, js[step - 2]
                                t.wait_ge(DVE.h, dve0 + ji + 1)
                                first, last_ = (ji == 0), (ji == n - 1)
                                for hh in range(2):
                                    t.matmul(out=ps_l[0:1, hh * 512:(hh + 1) * 512],
                                             lhsT=ones_b[:, :],
                                             rhs=probs[ji % 3][:, hh * 512:(hh + 1) * 512],
                                             start=first, stop=last_)
                                for hh in range(2):
                                    ins = t.matmul(
                                        out=ps_c[:, hh * 512:(hh + 1) * 512],
                                        lhsT=Vtm[:, (j * NKV + g) * 128:(j * NKV + g + 1) * 128],
                                        rhs=probs[ji % 3][:, hh * 512:(hh + 1) * 512],
                                        start=first, stop=last_)
                                PE.inc(ins)
                                rec[f"p3_cx_{seg}_{ji}"] = PE.n
                        rec[f"p3_mm_last_{seg}"] = PE.n
                        if inline_bc:
                            t.wait_ge(ACT.h, act0 + n + 2)
                            t.wait_ge(DVE.h, dve0 + n + 2)
                            for hh in range(2):
                                ins = t.matmul(out=ps_c[:, hh * 512:(hh + 1) * 512],
                                               lhsT=ones_fb[:, :],
                                               rhs=invl[0:1, hh * 512:(hh + 1) * 512],
                                               start=True, stop=True)
                            PE.inc(ins)
                            rec[f"p3_bc_{seg}"] = PE.n
                        else:
                            rec[f"p3_bc_{seg}"] = PE.n + 4
                            rec["p3_pending_bc"] = seg
                    block.tensor(tsec)

                    def mload(s, seg=seg, prev=prev, n=n, moff0=moff0):
                        if prev is not None:
                            s.wait_ge(DVE.h, rec[f"p3_mskend_{prev}"])
                        LD.inc(s.dma_start(out=mseg[:, 0:n * 512],
                                           in_=masksI[:, moff0 * 512:(moff0 + n) * 512]), 16)
                        rec[f"p3_ml_{seg}"] = LD.n
                    block.sync(mload)

                    def ssec(s, seg=seg, n=n):
                        act0 = rec[f"p3_act0_{seg}"]
                        assert ACT.n == act0
                        for ji in range(n):
                            s.wait_ge(PE.h, rec[f"p3_sc_{seg}_{ji}"])
                            if ji >= 3:
                                s.wait_ge(PE.h, rec[f"p3_cx_{seg}_{ji-3}"])
                            ACT.inc(s.activation(out=probs[ji % 3][:, :],
                                                 in_=ps_s[ji % 2][:, :],
                                                 func=AF.Exp, bias=0.0, scale=SM))
                            rec[f"p3_exp_{seg}_{ji}"] = ACT.n
                        rec[f"p3_exp_last_{seg}"] = ACT.n
                        s.wait_ge(PE.h, rec[f"p3_mm_last_{seg}"])
                        ACT.inc(s.activation(out=lntmp[0:1, 0:1024], in_=ps_l[:, :],
                                             func=AF.Ln, bias=0.0, scale=1.0))
                        rec[f"p3_lnl_{seg}"] = ACT.n
                        s.wait_ge(DVE.h, rec[f"p3_dve0_{seg}"] + n + 1)
                        ACT.inc(s.activation(out=invl[:, :], in_=lntmp[0:1, 1024:2048],
                                             func=AF.Exp, bias=0.0, scale=-1.0))
                        rec[f"p3_invl_{seg}"] = ACT.n
                        assert ACT.n == act0 + n + 2
                    block.scalar(ssec)

                    def vsec(v, seg=seg, prev=prev, n=n, moff0=moff0):
                        dve0 = rec[f"p3_dve0_{seg}"]
                        assert DVE.n == dve0
                        v.wait_ge(LD.h, rec[f"p3_ml_{seg}"])
                        for ji in range(n):
                            v.wait_ge(ACT.h, rec[f"p3_exp_{seg}_{ji}"])
                            for hh in range(2):
                                ins = v.tensor_tensor(
                                    out=probs[ji % 3][:, hh * 512:(hh + 1) * 512],
                                    in0=probs[ji % 3][:, hh * 512:(hh + 1) * 512],
                                    in1=mseg[:, ji * 512:(ji + 1) * 512],
                                    op=ALU.mult)
                            DVE.inc(ins)
                        rec[f"p3_mskend_{seg}"] = DVE.n
                        v.wait_ge(ACT.h, rec[f"p3_lnl_{seg}"])
                        DVE.inc(v.tensor_copy(out=lntmp[0:1, 1024:2048], in_=lntmp[0:1, 0:1024]))
                        v.wait_ge(PE.h, rec[f"p3_mm_last_{seg}"])
                        if prev is not None:
                            v.wait_ge(GP.h, rec[f"p3_st_{prev}"])
                        DVE.inc(v.tensor_copy(out=ctxe[:, :], in_=ps_c[:, :]))
                        rec[f"p3_ctxc_{seg}"] = DVE.n
                        assert DVE.n == dve0 + n + 2
                        v.wait_ge(PE.h, rec[f"p3_bc_{seg}"])
                        DVE.inc(v.tensor_tensor(out=ctxe[:, :], in0=ctxe[:, :],
                                                in1=ps_c[:, :], op=ALU.mult))
                        rec[f"p3_ctxs_{seg}"] = DVE.n
                    block.vector(vsec)

                    def gsec(gq, g=g, hp=hp, tq=tq, seg=seg):
                        gq.wait_ge(DVE.h, rec[f"p3_ctxs_{seg}"])
                        for hh in range(2):
                            h = g * QPG + 2 * hp + hh
                            GP.inc(gq.dma_start(
                                out=ctx_h.ap()[h * 128:(h + 1) * 128, tq * 512:(tq + 1) * 512],
                                in_=ctxe[:, hh * 512:(hh + 1) * 512]), 16)
                        rec[f"p3_st_{seg}"] = GP.n
                    block.gpsimd(gsec)
        rec["p3_pe_end"] = PE.n
        rec["p3_dve_end"] = DVE.n
        rec["p3_act_end"] = ACT.n
        rec["p3_gp_end"] = GP.n
    kvs.close()

    with contextlib.ExitStack() as p45:
        E4 = p45.enter_context
        y1 = E4(nc.sbuf_tensor("y1", [128, 32 * TPC], BF16))
        wdf = E4(nc.sbuf_tensor("wdf", [128, 4096], F32))
        wdb = [E4(nc.sbuf_tensor(f"wdb{i}", [128, 4096], BF16)) for i in range(2)]
        ysq = [E4(nc.sbuf_tensor(f"ysq{i}", [128, TPC], BF16)) for i in range(2)]
        tmpf = E4(nc.sbuf_tensor("tmpf", [128, TPC], F32))
        stat = E4(nc.sbuf_tensor("stat", [1, 3 * TPC], F32))

        def m_(i):
            return stat[0:1, i * TPC:(i + 1) * TPC]

        state = {}

        def dense_ffn(pfx, ctxbuf, ybuf, use_xres, bias_sb, wsrc, lora):
            with contextlib.ExitStack() as ls:
                ps_st = [ls.enter_context(nc.psum_tensor(f"ps_st_{pfx}{i}", [1, TPC], F32))
                         for i in range(2)]
                if lora:
                    with nc.psum_tensor("ps_h", [128, TPC], F32) as ps_h:
                        def hmm(t):
                            t.wait_ge(DVE.h, rec["p5_lprep"])
                            for half in range(2):
                                for dch in range(32):
                                    ins = t.matmul(
                                        out=ps_h[:, half * 512:(half + 1) * 512],
                                        lhsT=state["ATb"][:, dch * 128:(dch + 1) * 128],
                                        rhs=y1[:, dch * TPC + half * 512:
                                               dch * TPC + (half + 1) * 512],
                                        start=(dch == 0), stop=(dch == 31))
                            PE.inc(ins)
                            rec["p5_hmm"] = PE.n
                        block.tensor(hmm)

                        def hev(v):
                            v.wait_ge(PE.h, rec["p5_hmm"])
                            DVE.inc(v.tensor_scalar(out=state["hb"][:, :], in0=ps_h[:, :],
                                                    scalar1=coef_sb[:, 0:1], scalar2=None,
                                                    op0=ALU.mult))
                            rec["p5_h"] = DVE.n
                        block.vector(hev)

                with contextlib.ExitStack() as ds:
                    ps_d = [ds.enter_context(nc.psum_tensor(f"ps_d_{pfx}{i}", [128, TPC], F32))
                            for i in range(2)]
                    for og in range(32):
                        def wl(s, og=og):
                            if og == 0 and pfx == "p5":
                                s.wait_ge(DVE.h, rec["p5_lprep"])
                            if og >= 1:
                                s.wait_ge(ACT.h, rec[f"{pfx}_wc_{og-1}"])
                            slab1(s, wdf, wsrc, og * 128)
                            rec[f"{pfx}_wl_{og}"] = LD.n
                            if use_xres:
                                if og >= 2:
                                    s.wait_ge(DVE.h, rec[f"{pfx}_stt_{og-2}"])
                                LD.inc(s.dma_start(out=xres[og % 2][:, :],
                                                   in_=xown[og * 128:(og + 1) * 128, :]), 16)
                            rec[f"{pfx}_xl_{og}"] = LD.n
                        block.sync(wl)

                        def wc(s, og=og):
                            s.wait_ge(LD.h, rec[f"{pfx}_wl_{og}"])
                            if og >= 2:
                                s.wait_ge(PE.h, rec[f"{pfx}_mm_{og-2}"])
                            ACT.inc(s.activation(out=wdb[og % 2][:, :], in_=wdf[:, :],
                                                 func=AF.Identity, bias=0.0, scale=1.0))
                            rec[f"{pfx}_wc_{og}"] = ACT.n
                        block.scalar(wc)

                        def mm(t, og=og):
                            t.wait_ge(ACT.h, rec[f"{pfx}_wc_{og}"])
                            if og == 0:
                                if pfx == "p4":
                                    t.wait_ge(LD.h, rec["p4_ctx_ready"])
                                else:
                                    t.wait_ge(DVE.h, rec["p5_h"])
                            if og >= 2:
                                t.wait_ge(DVE.h, rec[f"{pfx}_stt_{og-2}"])
                            for half in range(2):
                                for dch in range(32):
                                    ins = t.matmul(
                                        out=ps_d[og % 2][:, half * 512:(half + 1) * 512],
                                        lhsT=wdb[og % 2][:, dch * 128:(dch + 1) * 128],
                                        rhs=ctxbuf[:, dch * TPC + half * 512:
                                                   dch * TPC + (half + 1) * 512],
                                        start=(dch == 0), stop=(dch == 31 and not lora))
                                if lora:
                                    ins = t.matmul(
                                        out=ps_d[og % 2][:, half * 512:(half + 1) * 512],
                                        lhsT=state["Bsc"][:, og * 128:(og + 1) * 128],
                                        rhs=state["hb"][:, half * 512:(half + 1) * 512],
                                        start=False, stop=True)
                            PE.inc(ins)
                            rec[f"{pfx}_mm_{og}"] = PE.n
                        block.tensor(mm)

                        def stt(v, og=og):
                            v.wait_ge(PE.h, rec[f"{pfx}_mm_{og}"])
                            if use_xres:
                                v.wait_ge(LD.h, rec[f"{pfx}_xl_{og}"])
                                res_ap = xres[og % 2][:, :]
                            else:
                                res_ap = y1[:, og * TPC:(og + 1) * TPC]
                            v.scalar_tensor_tensor(
                                out=ybuf[:, og * TPC:(og + 1) * TPC],
                                in0=ps_d[og % 2][:, :],
                                scalar=bias_sb[:, og:og + 1], op0=ALU.add,
                                in1=res_ap, op1=ALU.add)
                            DVE.inc(v.tensor_tensor(out=ysq[og % 2][:, :],
                                                    in0=ybuf[:, og * TPC:(og + 1) * TPC],
                                                    in1=ybuf[:, og * TPC:(og + 1) * TPC],
                                                    op=ALU.mult))
                            rec[f"{pfx}_stt_{og}"] = DVE.n
                        block.vector(stt)

                        def stm(t, og=og):
                            t.wait_ge(DVE.h, rec[f"{pfx}_stt_{og}"])
                            for half in range(2):
                                t.matmul(out=ps_st[0][0:1, half * 512:(half + 1) * 512],
                                         lhsT=ones_b[:, :],
                                         rhs=ybuf[:, og * TPC + half * 512:
                                                  og * TPC + (half + 1) * 512],
                                         start=(og == 0), stop=(og == 31))
                                ins = t.matmul(out=ps_st[1][0:1, half * 512:(half + 1) * 512],
                                               lhsT=ones_b[:, :],
                                               rhs=ysq[og % 2][:, half * 512:(half + 1) * 512],
                                               start=(og == 0), stop=(og == 31))
                            PE.inc(ins)
                            rec[f"{pfx}_st_{og}"] = PE.n
                        block.tensor(stm)

                def fin_v(v):
                    v.wait_ge(PE.h, rec[f"{pfx}_st_31"])
                    v.tensor_scalar(out=m_(0), in0=ps_st[0][:, :], scalar1=1.0 / D,
                                    scalar2=None, op0=ALU.mult)
                    v.tensor_scalar(out=m_(1), in0=ps_st[1][:, :], scalar1=1.0 / D,
                                    scalar2=EPS, op0=ALU.mult, op1=ALU.add)
                    v.tensor_tensor(out=m_(2), in0=m_(0), in1=m_(0), op=ALU.mult)
                    DVE.inc(v.tensor_tensor(out=m_(1), in0=m_(1), in1=m_(2), op=ALU.subtract))
                    rec[f"{pfx}_var"] = DVE.n
                block.vector(fin_v)

                def fin_s(s):
                    s.wait_ge(DVE.h, rec[f"{pfx}_var"])
                    ACT.inc(s.activation(out=m_(2), in_=m_(1), func=AF.Ln,
                                         bias=0.0, scale=1.0))
                    rec[f"{pfx}_ln"] = ACT.n
                block.scalar(fin_s)

                def fin_v2(v):
                    v.wait_ge(ACT.h, rec[f"{pfx}_ln"])
                    DVE.inc(v.tensor_copy(out=m_(1), in_=m_(2)))
                    rec[f"{pfx}_lncp"] = DVE.n
                block.vector(fin_v2)

                def fin_s2(s):
                    s.wait_ge(DVE.h, rec[f"{pfx}_lncp"])
                    ACT.inc(s.activation(out=m_(2), in_=m_(1), func=AF.Exp,
                                         bias=0.0, scale=-0.5))
                    rec[f"{pfx}_rstd"] = ACT.n
                block.scalar(fin_s2)

                with contextlib.ExitStack() as bs:
                    ps_bc = [bs.enter_context(nc.psum_tensor(f"ps_bc_{pfx}{i}", [128, TPC], F32))
                             for i in range(2)]

                    def bc(t):
                        t.wait_ge(ACT.h, rec[f"{pfx}_rstd"])
                        for half in range(2):
                            t.matmul(out=ps_bc[0][:, half * 512:(half + 1) * 512],
                                     lhsT=ones_f[:, :],
                                     rhs=m_(0)[0:1, half * 512:(half + 1) * 512],
                                     start=True, stop=True)
                            ins = t.matmul(out=ps_bc[1][:, half * 512:(half + 1) * 512],
                                           lhsT=ones_f[:, :],
                                           rhs=m_(2)[0:1, half * 512:(half + 1) * 512],
                                           start=True, stop=True)
                        PE.inc(ins)
                        rec[f"{pfx}_bc"] = PE.n
                    block.tensor(bc)

                    if pfx == "p4":
                        for og in range(32):
                            def nrm(v, og=og):
                                if og == 0:
                                    v.wait_ge(PE.h, rec["p4_bc"])
                                v.tensor_tensor(out=tmpf[:, :],
                                                in0=y1[:, og * TPC:(og + 1) * TPC],
                                                in1=ps_bc[0][:, :], op=ALU.subtract)
                                DVE.inc(v.tensor_tensor(out=y1[:, og * TPC:(og + 1) * TPC],
                                                        in0=tmpf[:, :],
                                                        in1=ps_bc[1][:, :], op=ALU.mult))
                                rec[f"p4_nrm_{og}"] = DVE.n
                            block.vector(nrm)
                        rec["p4_done"] = DVE.n
                    else:
                        for og in range(32):
                            def nrm(v, og=og):
                                if og == 0:
                                    v.wait_ge(PE.h, rec["p5_bc"])
                                if og >= 2:
                                    v.wait_ge(ST.h, rec[f"p5_out_{og-2}"])
                                v.tensor_tensor(out=state["outf"][og % 2][:, :],
                                                in0=ybuf[:, og * TPC:(og + 1) * TPC],
                                                in1=ps_bc[0][:, :], op=ALU.subtract)
                                DVE.inc(v.tensor_tensor(out=state["outf"][og % 2][:, :],
                                                        in0=state["outf"][og % 2][:, :],
                                                        in1=ps_bc[1][:, :], op=ALU.mult))
                                rec[f"p5_nrm_{og}"] = DVE.n
                            block.vector(nrm)

                            def ost(s, og=og):
                                s.wait_ge(DVE.h, rec[f"p5_nrm_{og}"])
                                ST.inc(s.dma_start(out=out_e[og * 128:(og + 1) * 128, :],
                                                   in_=state["outf"][og % 2][:, :]), 16)
                                rec[f"p5_out_{og}"] = ST.n
                            block.sync(ost)

        with contextlib.ExitStack() as cs:
            ctxb = cs.enter_context(nc.sbuf_tensor("ctxb", [128, 32 * TPC], BF16))
            xres = [cs.enter_context(nc.sbuf_tensor(f"xres{i}", [128, TPC], F32))
                    for i in range(2)]

            def ctxld(s):
                s.wait_ge(GP.h, rec["p3_gp_end"])
                s.wait_ge(DVE.h, rec["p3_dve_end"])
                s.wait_ge(PE.h, rec["p3_pe_end"])
                s.wait_ge(ACT.h, rec["p3_act_end"])
                for hh in range(32):
                    LD.inc(s.dma_start(out=ctxb[:, hh * TPC:(hh + 1) * TPC],
                                       in_=ctx_h.ap()[hh * 128:(hh + 1) * 128, :]), 16)
                rec["p4_ctx_ready"] = LD.n
            block.sync(ctxld)

            dense_ffn("p4", ctxb, y1, True, bd_sb, wdT, False)

        with contextlib.ExitStack() as fs:
            y2 = fs.enter_context(nc.sbuf_tensor("y2", [128, 32 * TPC], BF16))
            state["hb"] = fs.enter_context(nc.sbuf_tensor("hb", [128, TPC], BF16))
            state["ATb"] = fs.enter_context(nc.sbuf_tensor("ATb", [128, 32 * 128], BF16))
            state["Bsc"] = fs.enter_context(nc.sbuf_tensor("Bsc", [128, 4096], BF16))
            state["outf"] = [fs.enter_context(nc.sbuf_tensor(f"outf{i}", [128, TPC], F32))
                             for i in range(2)]

            def lprep_ld(s):
                s.wait_ge(ACT.h, rec["p4_wc_31"])
                s.wait_ge(DVE.h, rec["p4_done"])
                LD.inc(s.dma_start(
                    out=wdf[:, :].rearrange("p (c o) -> p c o", c=32),
                    in_=ATi[:, :].rearrange("(c p) o -> p c o", p=128)), 16)
                rec["p5_at_ld"] = LD.n
            block.sync(lprep_ld)

            with nc.psum_tensor("ps_misc", [128, 1], F32) as ps_misc:
                def lprep_pe(t):
                    t.wait_ge(LD.h, rec["p0"])
                    PE.inc(t.matmul(out=ps_misc[:, :], lhsT=sel_sb[:, :], rhs=lw_sb[:, :],
                                    start=True, stop=True))
                    rec["p5_coef_mm"] = PE.n
                block.tensor(lprep_pe)

                def lprep_v(v):
                    v.wait_ge(PE.h, rec["p5_coef_mm"])
                    DVE.inc(v.tensor_scalar(out=coef_sb[:, :], in0=ps_misc[:, :],
                                            scalar1=2.0, scalar2=None, op0=ALU.mult))
                    v.wait_ge(LD.h, rec["p5_at_ld"])
                    DVE.inc(v.tensor_copy(out=state["ATb"][:, :], in_=wdf[:, :]))
                    rec["p5_atb"] = DVE.n
                block.vector(lprep_v)

                def lprep_ld2(s):
                    s.wait_ge(DVE.h, rec["p5_atb"])
                    LD.inc(s.dma_start(out=wdf[:, :], in_=Bti[:, :]), 16)
                    rec["p5_bt_ld"] = LD.n
                block.sync(lprep_ld2)

                def lprep_v2(v):
                    v.wait_ge(LD.h, rec["p5_bt_ld"])
                    DVE.inc(v.tensor_copy(out=state["Bsc"][:, :], in_=wdf[:, :]))
                    rec["p5_lprep"] = DVE.n
                block.vector(lprep_v2)

            dense_ffn("p5", y1, y2, False, bf_sb, WTt, True)

            def fin(gq):
                gq.wait_ge(ST.h, rec["p5_out_31"])
            block.gpsimd(fin)

    st_.close()
    return nc


_NC_CACHE = None


def kernel(L0_attn_hidden_states, L0_attn_w_qkv, L0_attn_b_qkv, L0_attn_w_dense,
           L0_attn_b_dense, L0_ffn_lora_A, L0_ffn_lora_B, L0_ffn_W, L0_ffn_bias,
           L0_ffn_lora_weights):
    global _NC_CACHE
    x = np.asarray(L0_attn_hidden_states, dtype=np.float32)
    w_qkv = np.asarray(L0_attn_w_qkv, dtype=np.float32)
    b_qkv = np.asarray(L0_attn_b_qkv, dtype=np.float32)
    w_dense = np.asarray(L0_attn_w_dense, dtype=np.float32)
    b_dense = np.asarray(L0_attn_b_dense, dtype=np.float32)
    lora_A = np.asarray(L0_ffn_lora_A, dtype=np.float32)
    lora_B = np.asarray(L0_ffn_lora_B, dtype=np.float32)
    W = np.asarray(L0_ffn_W, dtype=np.float32)
    b_ffn = np.asarray(L0_ffn_bias, dtype=np.float32)
    lw = np.asarray(L0_ffn_lora_weights, dtype=np.float32)

    cos, sinS = rope_tables()
    wq_rows = np.concatenate([w_qkv[g * 768 + j * 128: g * 768 + (j + 1) * 128]
                              for g in range(NKV) for j in range(QPG)], axis=0)
    wk_rows = np.concatenate([w_qkv[g * 768 + 512: g * 768 + 640] for g in range(NKV)], axis=0)
    wv_rows = np.concatenate([w_qkv[g * 768 + 640: g * 768 + 768] for g in range(NKV)], axis=0)
    selm = np.zeros((8, 128), dtype=np.float32)
    for n in range(8):
        selm[n, n * 16:(n + 1) * 16] = 1.0
    common = dict(
        wqT=np.ascontiguousarray(wq_rows.T), wkT=np.ascontiguousarray(wk_rows.T),
        wvT=np.ascontiguousarray(wv_rows.T),
        wdT=np.ascontiguousarray(w_dense.T), WTt=np.ascontiguousarray(W.T),
        ATi=np.ascontiguousarray(lora_A.reshape(128, D).T),
        Bti=np.ascontiguousarray(lora_B.transpose(0, 2, 1).reshape(128, D)),
        lwI=np.ascontiguousarray(lw.reshape(8, 1)),
        bqI=np.ascontiguousarray(np.stack(
            [b_qkv[g * 768 + j * 128: g * 768 + (j + 1) * 128]
             for g in range(NKV) for j in range(QPG)], axis=1)),
        bkI=np.ascontiguousarray(np.stack(
            [b_qkv[g * 768 + 512: g * 768 + 640] for g in range(NKV)], axis=1)),
        bvI=np.ascontiguousarray(np.stack(
            [b_qkv[g * 768 + 640: g * 768 + 768] for g in range(NKV)], axis=1)),
        bdI=np.ascontiguousarray(b_dense.reshape(32, 128).T),
        bfI=np.ascontiguousarray(b_ffn.reshape(32, 128).T),
        onesbI=np.ones((128, 1), dtype=BF), onesfI=np.ones((1, 128), dtype=np.float32),
        onesfbI=np.ones((1, 128), dtype=BF),
        identI=np.eye(128, dtype=BF),
        selI=selm,
    )

    in_maps = []
    for c in range(NC_):
        b, q = c // 4, c % 4
        nb_pad, nreal, kpos = core_tables(q)
        xb_ = x[b]
        qoff = q * 1024
        m = dict(common)
        m["xneed"] = np.ascontiguousarray(xb_[kpos, :].T)
        m["xown"] = np.ascontiguousarray(xb_[qoff:qoff + 1024, :].T)
        m["cosqI"] = np.ascontiguousarray(cos[:, qoff:qoff + 1024].astype(BF))
        m["sinqI"] = np.ascontiguousarray(sinS[:, qoff:qoff + 1024].astype(BF))
        m["coskI"] = np.ascontiguousarray(cos[:, kpos].astype(BF))
        m["sinkI"] = np.ascontiguousarray(sinS[:, kpos].astype(BF))
        m["masksI"] = np.ascontiguousarray(mask_blob(q))
        in_maps.append(m)

    if _NC_CACHE is None:
        _NC_CACHE = build_nc()
    res = run_bass_kernel_spmd(_NC_CACHE, in_maps, core_ids=list(range(NC_)))
    outs = [np.asarray(res.results[c]["out_e"]).T for c in range(NC_)]
    return np.concatenate(outs, axis=0).reshape(B, S, D)
